# version 19
# baseline (speedup 1.0000x reference)
import sys
sys.path.insert(0, '/opt/trn_rl_repo')
import math
import numpy as np
import ml_dtypes
import concourse.bass as bass
import concourse.mybir as mybir
from concourse.bass_utils import run_bass_kernel_spmd

F32 = mybir.dt.float32
BF16 = mybir.dt.bfloat16
I8 = mybir.dt.int8
AF = mybir.ActivationFunctionType
ALU = mybir.AluOpType
bfd = ml_dtypes.bfloat16

D = 2048
NCH = 16
PCH = 8
S = 4096
B = 2
NOWN = 1024
CH = 256
CAPS = [4, 8, 12, 16]
NSLOT = 4
TOT_ITERS = sum(CAPS)
INTER = 8192
NIC = 64
GAMMA = 0.96875
EPS = 1e-5


def _valid_slots(j):
    return [s for s in range(NSLOT) if CAPS[s] > j]


ITER_IDX = {}
_t = 0
for _j in range(16):
    for _s in _valid_slots(_j):
        ITER_IDX[(_j, _s)] = _t
        _t += 1


class Sem:
    def __init__(self, nc, name):
        self.h = nc.alloc_semaphore(name)
        self.n = 0
        self.name = name


class EQ:
    def __init__(self, prog, key):
        self.prog = prog
        self.key = key

    def wait(self, sem, val):
        if val is None or val <= 0:
            return
        self.prog[self.key].append(lambda e, s=sem.h, v=val: e.wait_ge(s, v))

    def emit(self, fn, sem=None, k=1):
        if sem is None:
            self.prog[self.key].append(lambda e, f=fn: f(e))
            return None
        sem.n += k
        tgt = sem.n

        def run(e, f=fn, s=sem.h, kk=k):
            f(e).then_inc(s, kk)

        self.prog[self.key].append(run)
        return tgt


def build_nc(debug_taps=False):
    nc = bass.Bass("TRN2", target_bir_lowering=False, debug=False)

    x_own = nc.dram_tensor("x_own", [NOWN, D], F32, kind="ExternalInput").ap()
    x_seq = nc.dram_tensor("x_seq", [S, D], F32, kind="ExternalInput").ap()
    wq_i = nc.dram_tensor("wq", [D, D], F32, kind="ExternalInput").ap()
    wk_i = nc.dram_tensor("wk", [D, D], F32, kind="ExternalInput").ap()
    wv_i = nc.dram_tensor("wv", [D, D], F32, kind="ExternalInput").ap()
    w_gate = nc.dram_tensor("w_gate", [D, INTER], I8, kind="ExternalInput").ap()
    w_up = nc.dram_tensor("w_up", [D, INTER], I8, kind="ExternalInput").ap()
    w_down = nc.dram_tensor("w_down", [INTER, D], I8, kind="ExternalInput").ap()
    sg_col = nc.dram_tensor("sg_col", [128, NIC], F32, kind="ExternalInput").ap()
    su_col = nc.dram_tensor("su_col", [128, NIC], F32, kind="ExternalInput").ap()
    sd_b = nc.dram_tensor("sd_b", [128, D], F32, kind="ExternalInput").ap()
    ident_in = nc.dram_tensor("ident_in", [128, 128], BF16, kind="ExternalInput").ap()
    tqc_hi = nc.dram_tensor("tqc_hi", [1024, NOWN], F32, kind="ExternalInput").ap()
    tqs_hi = nc.dram_tensor("tqs_hi", [1024, NOWN], F32, kind="ExternalInput").ap()
    tqc_lo = nc.dram_tensor("tqc_lo", [1024, NOWN], BF16, kind="ExternalInput").ap()
    tqs_lo = nc.dram_tensor("tqs_lo", [1024, NOWN], BF16, kind="ExternalInput").ap()
    tkc_hi = nc.dram_tensor("tkc_hi", [1024, S], F32, kind="ExternalInput").ap()
    tks_hi = nc.dram_tensor("tks_hi", [1024, S], F32, kind="ExternalInput").ap()
    tkc_lo = nc.dram_tensor("tkc_lo", [1024, S], BF16, kind="ExternalInput").ap()
    tks_lo = nc.dram_tensor("tks_lo", [1024, S], BF16, kind="ExternalInput").ap()
    dt_hi = nc.dram_tensor("dt_hi", [TOT_ITERS * CH, CH], F32, kind="ExternalInput").ap()
    dt_lo = nc.dram_tensor("dt_lo", [TOT_ITERS * CH, CH], BF16, kind="ExternalInput").ap()

    y_out = nc.dram_tensor("y_out", [NOWN, D], F32, kind="ExternalOutput").ap()

    dkind = "ExternalOutput" if debug_taps else "Internal"
    xlnT_seq_d = nc.dram_tensor("xlnT_seq_d", [D, S], BF16, kind=dkind).ap()
    xlnT_own_d = nc.dram_tensor("xlnT_own_d", [D, NOWN], BF16, kind=dkind).ap()
    qt_d = nc.dram_tensor("qt_d", [D, NOWN], BF16, kind=dkind).ap()
    kt_d = nc.dram_tensor("kt_d", [D, S], BF16, kind=dkind).ap()
    v_d = nc.dram_tensor("v_d", [S, D], BF16, kind=dkind).ap()
    x2_d = nc.dram_tensor("x2_d", [NOWN, D], F32, kind=dkind).ap()
    x2lnT_d = nc.dram_tensor("x2lnT_d", [D, NOWN], BF16, kind=dkind).ap()
    if debug_taps:
        av_tap = nc.dram_tensor("av_tap", [8 * 128, D], F32, kind="ExternalOutput").ap()

    SP = Sem(nc, "SP")
    SV = Sem(nc, "SV")
    SA = Sem(nc, "SA")
    sems_all = [SP, SV, SA]

    def lanes(name, n):
        out = []
        for i in range(n):
            s_ = Sem(nc, f"{name}{i}")
            sems_all.append(s_)
            out.append(s_)
        return out

    ident = nc.alloc_sbuf_tensor("ident", [128, 128], BF16, side="right")
    sgc = nc.alloc_sbuf_tensor("sgc", [128, NIC], F32, side="right")
    suc = nc.alloc_sbuf_tensor("suc", [128, NIC], F32, side="right")
    sdb = nc.alloc_sbuf_tensor("sdb", [128, D], F32, side="right")
    epsb = nc.alloc_sbuf_tensor("epsb", [128, 1], F32, side="right")
    NR = 4
    st6 = nc.alloc_sbuf_tensor("st6", [128, 24 * NR], F32, side="right")
    ag2 = nc.alloc_sbuf_tensor("ag2", [128, 2 * NR], F32, side="right")
    sdv_ = nc.alloc_sbuf_tensor("sdv_", [128, NR], F32, side="right")
    rsd = nc.alloc_sbuf_tensor("rsd", [128, NR], F32, side="right")
    xs1 = nc.alloc_sbuf_tensor("xs1", [128, 512], F32, side="right")
    xs2 = nc.alloc_sbuf_tensor("xs2", [128, 512], F32, side="right")

    pb = {i: nc.alloc_psum_tensor(f"pb{i}", [128, 512], F32) for i in range(4)}
    pb[4] = None
    pb[5] = None
    pb[6] = None
    pb[7] = None
    pscopes = []

    def palloc(name, shape, dtype):
        cm = nc.psum_tensor(name, shape, dtype)
        h = cm.__enter__()
        pscopes.append((name, cm))
        return h

    def pfree(*names):
        for name in names:
            for i2, (n2, cm) in enumerate(pscopes):
                if n2 == name:
                    cm.__exit__(None, None, None)
                    pscopes.pop(i2)
                    break

    tpx = [None, None, None, None]

    prog = {k: [] for k in ("sy", "g", "v", "a", "t")}
    sy, g, v, a, t = (EQ(prog, k) for k in ("sy", "g", "v", "a", "t"))
    all_eq = [sy, g, v, a, t]

    bank_free = {b: None for b in range(8)}

    def bank_wait(eq, b):
        if bank_free[b] is not None:
            sem, tgt = bank_free[b]
            eq.wait(sem, tgt)

    _fence_snap = {}

    def fence():
        for eq in all_eq:
            for s_ in sems_all:
                if _fence_snap.get(s_.name) != s_.n:
                    eq.wait(s_, s_.n)
        for s_ in sems_all:
            _fence_snap[s_.name] = s_.n

    scopes = []

    def salloc(name, shape, dtype):
        cm = nc.sbuf_tensor(name, shape, dtype)
        h = cm.__enter__()
        scopes.append((name, cm))
        return h

    def sfree(*names):
        for name in names:
            for i, (n_, cm) in enumerate(scopes):
                if n_ == name:
                    cm.__exit__(None, None, None)
                    scopes.pop(i)
                    break

    slot_war = {r: 0 for r in range(NR)}

    def ln_stats(src_ap, r, pre_waits=()):
        v.wait(SV, slot_war[r])
        for (sem, tgt) in pre_waits:
            v.wait(sem, tgt)
        nchunk = src_ap.shape[-1] // 512
        for c_ in range(nchunk):
            v.emit(lambda e, s_=src_ap[:, c_ * 512:(c_ + 1) * 512], rr=r, c2=c_:
                   e.bn_stats(st6[:, 24 * rr + 6 * c2:24 * rr + 6 * c2 + 6], s_), SV, 1)
        v.wait(SV, SV.n)
        t1 = v.emit(lambda e, rr=r, n_=nchunk: e.bn_aggr(ag2[:, 2 * rr:2 * rr + 2],
                                                         st6[:, 24 * rr:24 * rr + 6 * n_]), SV, 1)
        a.wait(SV, t1)
        t2 = a.emit(lambda e, rr=r: e.activation(sdv_[:, rr:rr + 1], ag2[:, 2 * rr + 1:2 * rr + 2],
                                                 AF.Sqrt, bias=epsb[:]), SA, 1)
        v.wait(SA, t2)
        t3 = v.emit(lambda e, rr=r: e.reciprocal(rsd[:, rr:rr + 1], sdv_[:, rr:rr + 1]), SV, 1)
        return t3

    def ln_norm(dst_ap, src_ap, r, rec):
        v.wait(SV, rec)
        tgt = v.emit(lambda e, dd=dst_ap, s_=src_ap, rr=r:
                     e.tensor_scalar(dd, s_, ag2[:, 2 * rr:2 * rr + 1], rsd[:, rr:rr + 1],
                                     op0=ALU.subtract, op1=ALU.mult), SV, 1)
        slot_war[r] = tgt
        return tgt

    Linit = lanes("Li", 1)[0]
    sy.emit(lambda e: e.dma_start(out=ident[:], in_=ident_in), Linit, 16)
    sy.emit(lambda e: e.dma_start(out=sgc[:], in_=sg_col), Linit, 16)
    sy.emit(lambda e: e.dma_start(out=suc[:], in_=su_col), Linit, 16)
    sy.emit(lambda e: e.dma_start(out=sdb[:], in_=sd_b), Linit, 16)
    init_q = Linit.n
    eps_hi = float(np.float32(EPS))
    eps_lo = float(np.float32(EPS - eps_hi))
    v.emit(lambda e: e.memset(epsb[:], eps_hi), SV, 1)
    v.emit(lambda e: e.memset(xs1[:, 0:1], eps_lo), SV, 1)
    v.wait(SV, SV.n)
    eps_t = v.emit(lambda e: e.tensor_tensor(epsb[:], epsb[:], xs1[:, 0:1], op=ALU.add), SV, 1)
    slot_war[0] = eps_t

    W_sb = salloc("W_sb", [128, NCH * D], BF16)
    wv_ap = W_sb.ap().rearrange("p (c o) -> p c o", c=NCH)
    LW = lanes("LW", 4)
    w_loaded = {}

    def load_w(wap, key):
        qs_ = []
        for q4 in range(4):
            qs_.append(g.emit(lambda e, w=wap, q5=q4: e.dma_start(
                out=W_sb.ap().rearrange("p (c o) -> p c o", c=NCH)[:, q5 * 4:(q5 + 1) * 4],
                in_=w.rearrange("(c p) o -> p c o", p=128)[:, q5 * 4:(q5 + 1) * 4]), LW[q4], 16))
        w_loaded[key] = qs_

    load_w(wq_i, "q")

    xst = salloc("xst", [128, 4 * D], F32)
    xlnb = salloc("xlnb", [128, 4 * D], BF16)
    xtg = salloc("xtg", [128, 2 * NCH * 512], BF16)
    xtgv = xtg.ap().rearrange("p (gg c t) -> p gg c t", gg=2, c=NCH)
    Lx = lanes("Lx", 4)
    Lxt = lanes("Lxt", 2)
    for q_ in range(4):
        tpx[q_] = palloc(f"tp{q_}", [128, 1024], BF16)

    tiles = [(x_seq, i) for i in range(32)] + [(x_own, i) for i in range(8)]
    ld_t, ts_done, tp_half, ev_done, stq = {}, {}, {}, {}, {}
    aggr_t, sqrt_t = {}, {}
    t.wait(Linit, init_q)

    def p1_stage_a(i):
        xsrc, ti = tiles[i]
        sl = xst[:, (i % 4) * D:(i % 4 + 1) * D]
        if i >= 4:
            sy.wait(SV, ts_done[i - 4])
        ld_t[i] = sy.emit(lambda e, dd=sl, ss=xsrc[ti * 128:(ti + 1) * 128, :]:
                          e.dma_start(out=dd, in_=ss), Lx[i % 4], 16)
        r = i % NR
        v.wait(SV, slot_war[r])
        v.wait(Lx[i % 4], ld_t[i])
        for c_ in range(4):
            v.emit(lambda e, s_=sl[:, c_ * 512:(c_ + 1) * 512], rr=r, c2=c_:
                   e.bn_stats(st6[:, 24 * rr + 6 * c2:24 * rr + 6 * c2 + 6], s_), SV, 1)
        v.wait(SV, SV.n)
        aggr_t[i] = v.emit(lambda e, rr=r: e.bn_aggr(ag2[:, 2 * rr:2 * rr + 2],
                                                     st6[:, 24 * rr:24 * rr + 24]), SV, 1)
        a.wait(SV, aggr_t[i])
        sqrt_t[i] = a.emit(lambda e, rr=r: e.activation(sdv_[:, rr:rr + 1], ag2[:, 2 * rr + 1:2 * rr + 2],
                                                        AF.Sqrt, bias=epsb[:]), SA, 1)

    p1_stage_a(0)
    for i, (xsrc, ti) in enumerate(tiles):
        if i + 1 < len(tiles):
            p1_stage_a(i + 1)
        sl = xst[:, (i % 4) * D:(i % 4 + 1) * D]
        r = i % NR
        v.wait(SA, sqrt_t[i])
        rec = v.emit(lambda e, rr=r: e.reciprocal(rsd[:, rr:rr + 1], sdv_[:, rr:rr + 1]), SV, 1)
        xl = xlnb[:, (i % 4) * D:(i % 4 + 1) * D]
        if i >= 4:
            v.wait(SP, tp_half[(i - 4, 1)])
        ts_done[i] = ln_norm(xl, sl, r, rec)
        gslot = (i // 4) % 2
        tg = i % 4
        t.wait(SV, ts_done[i])
        if tg == 0 and i // 4 >= 2:
            t.wait(Lxt[gslot], stq[i // 4 - 2])
        for half in range(2):
            tb_ = (i * 2 + half) % 4
            bank_wait(t, 4 + tb_)
            for c in range(8):
                ch = half * 8 + c
                t.emit(lambda e, ph=tpx[tb_], cc=c, ss=xl[:, ch * 128:(ch + 1) * 128]:
                       e.transpose(ph[:, cc * 128:(cc + 1) * 128], ss, ident[:]),
                       SP if c == 7 else None, 1)
            tp_half[(i, half)] = SP.n
            a.wait(SP, tp_half[(i, half)])
            dst = xtgv[:, gslot, half * 8:half * 8 + 8, tg * 128:tg * 128 + 128]
            evt = a.emit(lambda e, dd=dst, ph=tpx[tb_]:
                         e.activation(dd, ph[:, 0:1024].rearrange("p (c t) -> p c t", c=8), AF.Copy),
                         SA, 1)
            bank_free[4 + tb_] = (SA, evt)
        ev_done[i] = SA.n
        if tg == 3:
            gidx = i // 4
            if gidx < 8:
                dstap = xlnT_seq_d.rearrange("(c p) t -> p c t", p=128)[:, :, gidx * 512:(gidx + 1) * 512]
            else:
                dstap = xlnT_own_d.rearrange("(c p) t -> p c t", p=128)[:, :, (gidx - 8) * 512:(gidx - 7) * 512]
            a.wait(SA, ev_done[i])
            stq[gidx] = a.emit(lambda e, dd=dstap, ss=xtgv[:, gslot]: e.dma_start(out=dd, in_=ss),
                               Lxt[gslot], 16)

    fence()
    sfree("xtg", "xlnb", "xst")
    pfree("tp3", "tp2", "tp1", "tp0")
    for q_ in range(4, 8):
        pb[q_] = palloc(f"pf{q_}", [128, 512], F32)

    xT = salloc("xT", [128, 2 * NCH * 512], BF16)
    tbuf = salloc("tbuf", [128, 2 * 2 * PCH * 512], F32)
    qst = salloc("qst", [128, 2 * 2 * 512], BF16)

    xtv_all = xT.ap().rearrange("p (r c t) -> p r c t", r=2, c=NCH)
    tbv = tbuf.ap().rearrange("p (r k c t) -> p r k c t", r=2, k=2, c=PCH)
    qsv = qst.ap().rearrange("p (r u t) -> p r u t", r=2, u=2)
    LxT = lanes("LxT", 2)
    Ltb = lanes("Ltb", 2)
    Lqs = lanes("Lqs", 2)
    qst_last = {0: None, 1: None}
    tb_last = {0: None, 1: None, 2: None}
    xT_last = {0: None, 1: None}

    def proj_qk(n_groups, xlnT_src, tabs, out_d, wkey):
        tc_hi, ts_hi, tc_lo, ts_lo = tabs
        xt_t, tb_t, st_t = {}, {}, {}
        pe_grp, dv_grp = {}, {}
        for gi in range(n_groups):
            rs = gi % 2
            if xT_last[rs] is not None:
                sy.wait(SP, xT_last[rs])
            xt_t[gi] = sy.emit(lambda e, r=rs,
                               ss=xlnT_src.rearrange("(c p) t -> p c t", p=128)[:, :, gi * 512:(gi + 1) * 512]:
                               e.dma_start(out=xtv_all[:, r], in_=ss), LxT[rs], 16)
            rt = gi % 2
            if tb_last[rt] is not None:
                g.wait(SV, tb_last[rt])
            hi_ts = []
            for kk in range(2):
                thi = tc_hi if kk == 0 else ts_hi
                hi_ts.append(g.emit(lambda e, r=rt, k2=kk,
                                    ss=thi.rearrange("(c p) t -> p c t", p=128)[:, :, gi * 512:(gi + 1) * 512]:
                                    e.dma_start(out=tbv[:, r, k2], in_=ss), Ltb[rt], 16))
            g.wait(Ltb[rt], hi_ts[1])
            for kk in range(2):
                tlo = tc_lo if kk == 0 else ts_lo
                tb_t[gi] = g.emit(lambda e, r=rt, k2=kk,
                                  ss=tlo.rearrange("(c p) t -> p c t", p=128)[:, :, gi * 512:(gi + 1) * 512]:
                                  e.dma_start(out=tbv[:, r, k2], in_=ss, accum_op=ALU.add), Ltb[rt], 16)
            if gi == 0:
                pass
            else:
                t.wait(LW[3], w_loaded[wkey][3])
            t.wait(LxT[rs], xt_t[gi])
            for p_ in range(8):
                be, bo = 2 * (p_ % 3), 2 * (p_ % 3) + 1
                qr = p_ % 2
                bank_wait(t, be)
                bank_wait(t, bo)
                for c in range(NCH):
                    if gi == 0 and p_ == 0 and c % 4 == 0:
                        t.wait(LW[c // 4], w_loaded[wkey][c // 4])
                    t.emit(lambda e, b=be, cc=c, pp=p_, r=rs:
                           e.matmul(pb[b][:], wv_ap[:, cc, pp * 128:(pp + 1) * 128], xtv_all[:, r, cc],
                                    start=(cc == 0), stop=(cc == NCH - 1)))
                for c in range(NCH):
                    t.emit(lambda e, b=bo, cc=c, pp=p_, r=rs:
                           e.matmul(pb[b][:], wv_ap[:, cc, (pp + 8) * 128:(pp + 9) * 128], xtv_all[:, r, cc],
                                    start=(cc == 0), stop=(cc == NCH - 1)),
                           SP if c == NCH - 1 else None, 1)
                pe_t = SP.n
                v.wait(SP, pe_t)
                v.wait(Ltb[rt], tb_t[gi])
                if qst_last[qr] is not None:
                    v.wait(Lqs[qr], qst_last[qr])
                v.wait(SV, SV.n)
                v.emit(lambda e, b=be, r=rt, pp=p_: e.tensor_tensor(xs1[:], pb[b][:], tbv[:, r, 0, pp], op=ALU.mult), SV, 1)
                v.emit(lambda e, b=bo, r=rt, pp=p_: e.tensor_tensor(xs2[:], pb[b][:], tbv[:, r, 1, pp], op=ALU.mult), SV, 1)
                v.wait(SV, SV.n)
                v.emit(lambda e, q2=qr: e.tensor_tensor(qsv[:, q2, 0], xs1[:], xs2[:], op=ALU.subtract), SV, 1)
                v.wait(SV, SV.n)
                v.emit(lambda e, b=bo, r=rt, pp=p_: e.tensor_tensor(xs1[:], pb[b][:], tbv[:, r, 0, pp], op=ALU.mult), SV, 1)
                v.emit(lambda e, b=be, r=rt, pp=p_: e.tensor_tensor(xs2[:], pb[b][:], tbv[:, r, 1, pp], op=ALU.mult), SV, 1)
                v.wait(SV, SV.n)
                dvp = v.emit(lambda e, q2=qr: e.tensor_tensor(qsv[:, q2, 1], xs1[:], xs2[:], op=ALU.add), SV, 1)
                bank_free[be] = (SV, dvp)
                bank_free[bo] = (SV, dvp)
                a.wait(SV, dvp)
                dpair = out_d.rearrange("(h f p) t -> p h f t", p=128, h=2)[:, :, p_, gi * 512:(gi + 1) * 512]
                st_t[(gi, p_)] = a.emit(lambda e, q2=qr, dd=dpair:
                                        e.dma_start(out=dd, in_=qsv[:, q2]), Lqs[qr], 16)
                qst_last[qr] = st_t[(gi, p_)]
            dv_grp[gi] = SV.n
            pe_grp[gi] = SP.n
            tb_last[rt] = SV.n
            xT_last[rs] = SP.n
        return st_t

    proj_qk(2, xlnT_own_d, (tqc_hi, tqs_hi, tqc_lo, tqs_lo), qt_d, "q")
    g.wait(SP, SP.n)
    load_w(wk_i, "k")
    proj_qk(8, xlnT_seq_d, (tkc_hi, tks_hi, tkc_lo, tks_lo), kt_d, "k")
    g.wait(SP, SP.n)
    load_w(wv_i, "v")

    fence()
    sfree("qst", "tbuf")
    vst = salloc("vst", [128, 3 * 4 * D], BF16)
    vsv = vst.ap().rearrange("p (r u d) -> p r u d", r=3, u=4)
    Lvs = lanes("Lvs", 3)
    v_xt, v_st, v_pe_grp = {}, {}, {}
    for gi in range(8):
        rs = gi % 2
        if xT_last[rs] is not None:
            sy.wait(SP, xT_last[rs])
        v_xt[gi] = sy.emit(lambda e, r=rs,
                           ss=xlnT_seq_d.rearrange("(c p) t -> p c t", p=128)[:, :, gi * 512:(gi + 1) * 512]:
                           e.dma_start(out=xtv_all[:, r], in_=ss), LxT[rs], 16)
        if gi > 0:
            t.wait(LW[3], w_loaded["v"][3])
        t.wait(LxT[rs], v_xt[gi])
        first_ev = True
        for ts_ in range(4):
            for j in range(4):
                bk = (ts_ * 4 + j) % 6
                bank_wait(t, bk)
                for c in range(NCH):
                    if gi == 0 and ts_ == 0 and j == 0 and c % 4 == 0:
                        t.wait(LW[c // 4], w_loaded["v"][c // 4])
                    t.emit(lambda e, b=bk, cc=c, tt=ts_, jj=j, r=rs:
                           e.matmul(pb[b][:], xtv_all[:, r, cc, tt * 128:(tt + 1) * 128],
                                    wv_ap[:, cc, jj * 512:(jj + 1) * 512],
                                    start=(cc == 0), stop=(cc == NCH - 1)),
                           SP if c == NCH - 1 else None, 1)
                a.wait(SP, SP.n)
                if first_ev and gi >= 3:
                    a.wait(Lvs[gi % 3], v_st[gi - 3])
                first_ev = False
                evt = a.emit(lambda e, b=bk, tt=ts_, jj=j, r=gi % 3:
                             e.activation(vsv[:, r, tt, jj * 512:(jj + 1) * 512], pb[b][:], AF.Copy), SA, 1)
                bank_free[bk] = (SA, evt)
        v_pe_grp[gi] = SP.n
        xT_last[rs] = SP.n
        a.wait(SA, SA.n)
        v_st[gi] = a.emit(lambda e, r=gi % 3, dd=v_d.rearrange("(u p) d -> p u d", p=128)[:, gi * 4:(gi + 1) * 4, :]:
                          e.dma_start(out=dd, in_=vsv[:, r]), Lvs[gi % 3], 16)

    sy.wait(SP, SP.n)
    g.wait(SP, SP.n)
    v.wait(SP, SP.n)
    for lv in list(Lvs) + list(Lqs):
        sy.wait(lv, lv.n)
        g.wait(lv, lv.n)
    sfree("vst", "xT", "W_sb")

    qT_sb = salloc("qT_sb", [128, NSLOT * NCH * CH], BF16)
    att_sb = salloc("att_sb", [128, TOT_ITERS * 2 * CH], BF16)
    ktT = salloc("ktT", [128, 2 * NCH * CH], BF16)
    Dcb = salloc("Dcb", [128, 4 * 4 * 2 * CH], F32)
    ktv_all = ktT.ap().rearrange("p (r c t) -> p r c t", r=2, c=NCH)
    dcv = Dcb.ap().rearrange("p (r n u q) -> p r n u q", r=4, n=4, u=2)
    qv_all = qT_sb.ap().rearrange("p (c s t) -> p c s t", c=NCH, s=NSLOT)
    attv = att_sb.ap().rearrange("p (t u q) -> p t u q", t=TOT_ITERS, u=2)
    Lq = lanes("Lq", 1)[0]
    Lkt = lanes("Lkt", 2)
    Ldc = lanes("Ldc", 4)

    for s_ in range(NSLOT):
        sy.emit(lambda e, ss2=s_,
                ss=qt_d.rearrange("(c p) t -> p c t", p=128)[:, :, s_ * CH:(s_ + 1) * CH]:
                e.dma_start(out=qv_all[:, :, ss2], in_=ss), Lq, 16)
    qld_all = Lq.n
    a3_pe, a3_dv, kt_t, dc_t = {}, {}, {}, {}
    for j in range(16):
        rs = j % 2
        vs = _valid_slots(j)
        if j >= 2:
            sy.wait(SP, a3_pe[j - 2])
        kt_t[j] = sy.emit(lambda e, r=rs,
                          ss=kt_d.rearrange("(c p) t -> p c t", p=128)[:, :, j * CH:(j + 1) * CH]:
                          e.dma_start(out=ktv_all[:, r], in_=ss), Lkt[rs], 16)
        off = ITER_IDX[(j, vs[0])]
        nv = len(vs)
        rd = j % 4
        if j >= 4:
            g.wait(SV, a3_dv[j - 4])
        hi_t = g.emit(lambda e, r=rd, n_=nv,
                      ss=dt_hi[off * CH:(off + nv) * CH, :].rearrange("(n u p) q -> p n u q", p=128, u=2):
                      e.dma_start(out=dcv[:, r, 0:n_], in_=ss), Ldc[rd], 16)
        g.wait(Ldc[rd], hi_t)
        dc_t[j] = g.emit(lambda e, r=rd, n_=nv,
                         ss=dt_lo[off * CH:(off + nv) * CH, :].rearrange("(n u p) q -> p n u q", p=128, u=2):
                         e.dma_start(out=dcv[:, r, 0:n_], in_=ss, accum_op=ALU.add), Ldc[rd], 16)
        t.wait(Lkt[rs], kt_t[j])
        if j == 0:
            t.wait(Lq, qld_all)
        pairs = [vs[i2:i2 + 2] for i2 in range(0, len(vs), 2)]
        bkc = [0]
        for pr in pairs:
            s0 = pr[0]
            i0 = vs.index(s0)
            w_ = len(pr)
            for u in range(2):
                bk = bkc[0] % 6
                bkc[0] += 1
                bank_wait(t, bk)
                for c in range(NCH):
                    t.emit(lambda e, b=bk, cc=c, r=rs, uu=u, ss2=s0, w2=w_:
                           e.matmul(pb[b][:, 0:w2 * CH], ktv_all[:, r, cc, uu * 128:(uu + 1) * 128],
                                    qv_all[:, cc, ss2:ss2 + w2].rearrange("p s t -> p (s t)"),
                                    start=(cc == 0), stop=(cc == NCH - 1)),
                           SP if c == NCH - 1 else None, 1)
                v.wait(SP, SP.n)
                v.wait(Ldc[rd], dc_t[j])
                evt = None
                for w2 in range(w_):
                    evt = v.emit(lambda e, b=bk, r=rd, n_=i0 + w2, uu=u, tx=off + i0 + w2, o2=w2:
                                 e.tensor_tensor(attv[:, tx, uu], pb[b][:, o2 * CH:(o2 + 1) * CH],
                                                 dcv[:, r, n_, uu], op=ALU.mult), SV, 1)
                bank_free[bk] = (SV, evt)
        a3_pe[j] = SP.n
        a3_dv[j] = SV.n

    sy.wait(SP, SP.n)
    sy.wait(SV, SV.n)
    g.wait(SP, SP.n)
    a.wait(SV, SV.n)
    sfree("Dcb", "ktT")

    av_sb = salloc("av_sb", [128, 8 * D], F32)
    vb = salloc("vb", [128, 4 * 4 * 512], BF16)
    vbv = vb.ap().rearrange("p (r u d) -> p r u d", r=4, u=4)
    avv = av_sb.ap().rearrange("p (i d) -> p i d", i=8)
    Lvb = lanes("Lvb", 4)
    b_ld, b_pe, b_ac = {}, {}, {}
    for dq in range(4):
        for jp in range(8):
            slot = jp % 4
            idx = dq * 8 + jp
            if idx >= 4:
                sy.wait(SP, b_pe[idx - 4])
            b_ld[idx] = sy.emit(lambda e, sl=slot,
                                ss=v_d.rearrange("(u p) d -> p u d", p=128)[:, jp * 4:(jp + 1) * 4, dq * 512:(dq + 1) * 512]:
                                e.dma_start(out=vbv[:, sl], in_=ss), Lvb[slot], 16)
            t.wait(Lvb[slot], b_ld[idx])
            if dq == 0 and jp == 0:
                for b_ in range(8):
                    bank_wait(t, b_)
            if dq > 0 and jp == 0:
                t.wait(SA, b_ac[dq - 1])
            lastj = 2 * jp + 1
            for j in (2 * jp, 2 * jp + 1):
                for s_ in _valid_slots(j):
                    tix = ITER_IDX[(j, s_)]
                    for u in range(2):
                        for qs in range(2):
                            last = (j == lastj and s_ == _valid_slots(lastj)[-1] and u == 1 and qs == 1)
                            t.emit(lambda e, ph=pb[s_ * 2 + qs], tx=tix, uu=u, q2=qs, sl=slot, jj=j, ss2=s_:
                                   e.matmul(ph[:], attv[:, tx, uu, q2 * 128:(q2 + 1) * 128],
                                            vbv[:, sl, (jj % 2) * 2 + uu],
                                            start=(jj == 0 and uu == 0), stop=(jj == CAPS[ss2] - 1 and uu == 1)),
                                   SP if last else None, 1)
            b_pe[idx] = SP.n
        a.wait(SP, b_pe[dq * 8 + 7])
        for i8_ in range(8):
            a.emit(lambda e, ii=i8_, qq=dq, ph=pb[i8_]: e.activation(avv[:, ii, qq * 512:(qq + 1) * 512], ph[:], AF.Copy),
                   SA, 1)
        b_ac[dq] = SA.n
    for b_ in range(8):
        bank_free[b_] = (SA, b_ac[3])
    if debug_taps:
        Ltap = lanes("Ltap", 1)[0]
        sy.wait(SA, b_ac[3])
        sy.emit(lambda e: e.dma_start(out=av_tap.rearrange("(i p) d -> p i d", p=128), in_=avv[:, :, :]), Ltap, 16)

    sy.wait(SP, SP.n)
    v.wait(SP, SP.n)
    a.wait(SP, SP.n)
    sfree("vb")
    pfree("pf7", "pf6", "pf5", "pf4")
    for q_ in range(4):
        tpx[q_] = palloc(f"tq{q_}", [128, 1024], BF16)

    xol = salloc("xol", [128, 2 * D], F32)
    x2b = salloc("x2b", [128, 2 * D], F32)
    x2l = salloc("x2l", [128, 2 * D], BF16)
    x2s = salloc("x2s", [128, 2 * NCH * 128], BF16)
    x2sv = x2s.ap().rearrange("p (r c t) -> p r c t", r=2, c=NCH)
    Lxo = lanes("Lxo", 2)
    Lx2 = lanes("Lx2", 2)
    Lfm = lanes("Lfm", 2)
    p5_dv, p5_pe, p5_st, p5_ac, p5_fmst = {}, {}, {}, {}, {}
    for i in range(8):
        r2 = i % 2
        r = i % NR
        if i >= 2:
            sy.wait(SV, p5_dv[i - 2])
        ld = sy.emit(lambda e, dd=xol[:, r2 * D:(r2 + 1) * D], ss=x_own[i * 128:(i + 1) * 128, :]:
                     e.dma_start(out=dd, in_=ss), Lxo[r2], 16)
        avt = avv[:, i, :]
        rec = ln_stats(avt, r, pre_waits=[(SA, b_ac[3])])
        if i >= 2:
            v.wait(Lx2[r2], p5_st[i - 2])
        t_ln2 = ln_norm(x2b[:, r2 * D:(r2 + 1) * D], avt, r, rec)
        v.wait(Lxo[r2], ld)
        v.wait(SV, t_ln2)
        t_x2 = v.emit(lambda e, dd=x2b[:, r2 * D:(r2 + 1) * D],
                      s2=xol[:, r2 * D:(r2 + 1) * D]: e.tensor_tensor(dd, dd, s2, op=ALU.add), SV, 1)
        g.wait(SV, t_x2)
        p5_st[i] = g.emit(lambda e, dd=x2_d[i * 128:(i + 1) * 128, :], ss=x2b[:, r2 * D:(r2 + 1) * D]:
                          e.dma_start(out=dd, in_=ss), Lx2[r2], 16)
        rec3 = ln_stats(x2b[:, r2 * D:(r2 + 1) * D], r, pre_waits=[(SV, t_x2)])
        if i >= 2:
            v.wait(SP, p5_pe[i - 2])
        p5_dv[i] = ln_norm(x2l[:, r2 * D:(r2 + 1) * D], x2b[:, r2 * D:(r2 + 1) * D], r, rec3)
        t.wait(SV, p5_dv[i])
        if i >= 2:
            a.wait(Lfm[r2], p5_fmst[i - 2])
        for half in range(2):
            tb_ = (i * 2 + half) % 4
            bank_wait(t, 4 + tb_)
            for c in range(8):
                ch = half * 8 + c
                t.emit(lambda e, ph=tpx[tb_], cc=c, ss=x2l[:, r2 * D + ch * 128:r2 * D + (ch + 1) * 128]:
                       e.transpose(ph[:, cc * 128:(cc + 1) * 128], ss, ident[:]),
                       SP if c == 7 else None, 1)
            php = SP.n
            a.wait(SP, php)
            dst = x2sv[:, r2, half * 8:half * 8 + 8, :]
            evt = a.emit(lambda e, dd=dst, ph=tpx[tb_]:
                         e.activation(dd, ph[:, 0:1024].rearrange("p (c t) -> p c t", c=8), AF.Copy), SA, 1)
            bank_free[4 + tb_] = (SA, evt)
        p5_pe[i] = SP.n
        p5_ac[i] = SA.n
        g.wait(SA, p5_ac[i])
        p5_fmst[i] = g.emit(lambda e, dd=x2lnT_d.rearrange("(c p) t -> p c t", p=128)[:, :, i * 128:(i + 1) * 128],
                            ss=x2sv[:, r2]: e.dma_start(out=dd, in_=ss), Lfm[r2], 16)

    fence()
    sfree("x2s", "x2l", "x2b", "xol", "av_sb", "att_sb", "qT_sb")
    pfree("tq3", "tq2", "tq1", "tq0")
    for q_ in range(4, 8):
        pb[q_] = palloc(f"pg{q_}", [128, 512], F32)

    GI = 256
    fuse = salloc("fuse", [128, NIC * 512], BF16)
    wgu = salloc("wgu", [128, 2 * 2 * NCH * GI], BF16)
    wdb = salloc("wdb", [128, 2 * 2 * 2 * 512], BF16)
    x2T = salloc("x2T", [128, NCH * 512], BF16)
    gsb = salloc("gsb", [128, 2 * 512], F32)
    usb = salloc("usb", [128, 2 * 512], F32)
    ffb = salloc("ffb", [128, 4 * D], F32)
    x2r = salloc("x2r", [128, 2 * D], F32)
    yb = salloc("yb", [128, 2 * D], F32)

    wguv = wgu.ap().rearrange("p (w r c i) -> p w r c i", w=2, r=2, c=NCH)
    wdv = wdb.ap().rearrange("p (r n u d) -> p r n u d", r=2, n=2, u=2)
    fusev = fuse.ap().rearrange("p (i t) -> p i t", i=NIC)
    ffv = ffb.ap().rearrange("p (u d) -> p u d", u=4)
    x2Tv = x2T.ap().rearrange("p (c t) -> p c t", c=NCH)
    Lgu = lanes("Lgu", 2)
    Lwd = lanes("Lwd", 2)
    Lxx = lanes("Lxx", 1)[0]
    Lxr = lanes("Lxr", 2)
    Ly = lanes("Ly", 2)

    f_w, f_dv, f_pe, f_ev, y_st, ydv = {}, {}, {}, {}, {}, {}
    x2T_t = {}
    for cki in range(2):
        if cki >= 1:
            sy.wait(SP, f_pe[("f1", cki * 64 - 1)])
        x2T_t[cki] = sy.emit(lambda e, ss=x2lnT_d.rearrange("(c p) t -> p c t", p=128)[:, :, cki * 512:(cki + 1) * 512]:
                             e.dma_start(out=x2Tv[:, :, :], in_=ss), Lxx, 16)
        for icg in range(32):
            gic = cki * 32 + icg
            rs = icg % 2
            if gic >= 2:
                g.wait(SP, f_pe[("f1", (gic - 2) * 2 + 1)])
            io = icg * GI
            g.emit(lambda e, r=rs, ss=w_gate.rearrange("(c p) i -> p c i", p=128)[:, :, io:io + GI]:
                   e.dma_start(out=wguv[:, 0, r], in_=ss), Lgu[rs], 16)
            f_w[gic] = g.emit(lambda e, r=rs, ss=w_up.rearrange("(c p) i -> p c i", p=128)[:, :, io:io + GI]:
                              e.dma_start(out=wguv[:, 1, r], in_=ss), Lgu[rs], 16)
            for ii in range(2):
                ic = icg * 2 + ii
                gfull = cki * 64 + ic
                bg, bu = (4, 5) if ic % 2 == 0 else (2, 3)
                t.wait(Lgu[rs], f_w[gic])
                t.wait(Lxx, x2T_t[cki])
                for bkk, wsel in ((bg, 0), (bu, 1)):
                    bank_wait(t, bkk)
                    for c in range(NCH):
                        t.emit(lambda e, b=bkk, w=wsel, r=rs, cc=c, i2=ii:
                               e.matmul(pb[b][:], wguv[:, w, r, cc, i2 * 128:(i2 + 1) * 128],
                                        x2Tv[:, cc, :], start=(cc == 0), stop=(cc == NCH - 1)),
                               SP if c == NCH - 1 else None, 1)
                f_pe[("f1", gic * 2 + ii)] = SP.n
                rr = ic % 2
                a.wait(SP, SP.n)
                if gfull >= 2:
                    a.wait(SV, f_dv[gfull - 2])
                ga = a.emit(lambda e, r2_=rr, icc=ic, b=bg: e.activation(gsb[:, r2_ * 512:(r2_ + 1) * 512], pb[b][:],
                                                                   AF.Silu, scale=sgc[:, icc:icc + 1]), SA, 1)
                bank_free[bg] = (SA, ga)
                v.wait(SP, SP.n)
                if gfull >= 2:
                    v.wait(SV, f_dv[gfull - 2])
                ua = v.emit(lambda e, r2_=rr, icc=ic, b=bu: e.tensor_scalar(usb[:, r2_ * 512:(r2_ + 1) * 512], pb[b][:],
                                                                      suc[:, icc:icc + 1], None, op0=ALU.mult), SV, 1)
                bank_free[bu] = (SV, ua)
                v.wait(SA, ga)
                v.wait(SV, ua)
                if cki >= 1 and ic < 2:
                    v.wait(SP, f_pe[("f2", cki - 1)])
                f_dv[gfull] = v.emit(lambda e, r2_=rr, icc=ic:
                                     e.tensor_tensor(fusev[:, icc], gsb[:, r2_ * 512:(r2_ + 1) * 512],
                                                     usb[:, r2_ * 512:(r2_ + 1) * 512], op=ALU.mult), SV, 1)
        for h in range(2):
            for icp in range(NIC // 2):
                gdn = (cki * 2 + h) * (NIC // 2) + icp
                rs = icp % 2
                if gdn >= 2:
                    g.wait(SP, f_pe[("f2g", gdn - 2)])
                f_w[("d", gdn)] = g.emit(
                    lambda e, r=rs,
                    ss=w_down[icp * 256:(icp + 1) * 256, h * 1024:(h + 1) * 1024].rearrange("(n p) (u d) -> p n u d", p=128, u=2):
                    e.dma_start(out=wdv[:, r], in_=ss), Lwd[rs], 16)
                t.wait(Lwd[rs], f_w[("d", gdn)])
                if icp == 0:
                    t.wait(SV, f_dv[cki * 64 + NIC - 1])
                    for b_ in range(8):
                        bank_wait(t, b_)
                for nn_ in range(2):
                    ic = icp * 2 + nn_
                    for ts_ in range(4):
                        for dc in range(2):
                            t.emit(lambda e, ph=pb[ts_ * 2 + dc], icc=ic, tt=ts_, d2=dc, r=rs, n2=nn_:
                                   e.matmul(ph[:], fusev[:, icc, tt * 128:(tt + 1) * 128], wdv[:, r, n2, d2],
                                            start=(icc == 0), stop=(icc == NIC - 1)),
                                   SP if (ts_ == 3 and dc == 1 and nn_ == 1) else None, 1)
                f_pe[("f2g", gdn)] = SP.n
            f_pe[("f2", cki)] = SP.n
            v.wait(SP, SP.n)
            for ts_ in range(4):
                for dc in range(2):
                    if cki == 1 and h == 0 and ts_ == 0 and dc == 0:
                        v.wait(SV, ydv.get(3, 0))
                    evt = v.emit(lambda e, tt=ts_, d2=dc, hh=h, ph=pb[ts_ * 2 + dc]:
                                 e.tensor_tensor(ffv[:, tt, hh * 1024 + d2 * 512:hh * 1024 + (d2 + 1) * 512],
                                                 ph[:],
                                                 sdb[:, hh * 1024 + d2 * 512:hh * 1024 + (d2 + 1) * 512],
                                                 op=ALU.mult), SV, 1)
                    bank_free[ts_ * 2 + dc] = (SV, evt)
            f_ev[(cki, h)] = SV.n
        for ts_ in range(4):
            i = cki * 4 + ts_
            r = ts_ % NR
            r2 = ts_ % 2
            sy.wait(Lx2[i % 2], p5_st[i])
            sy.wait(SV, ydv.get(i - 2, 0))
            ld = sy.emit(lambda e, dd=x2r[:, r2 * D:(r2 + 1) * D], ss=x2_d[i * 128:(i + 1) * 128, :]:
                         e.dma_start(out=dd, in_=ss), Lxr[r2], 16)
            rec = ln_stats(ffv[:, ts_, :], r, pre_waits=[(SV, f_ev[(cki, 1)])])
            if i >= 2:
                v.wait(Ly[r2], y_st[i - 2])
            t_n = ln_norm(yb[:, r2 * D:(r2 + 1) * D], ffv[:, ts_, :], r, rec)
            v.wait(Lxr[r2], ld)
            v.wait(SV, t_n)
            ydv[i] = v.emit(lambda e, dd=yb[:, r2 * D:(r2 + 1) * D], s2=x2r[:, r2 * D:(r2 + 1) * D]:
                            e.tensor_tensor(dd, dd, s2, op=ALU.add), SV, 1)
            a.wait(SV, ydv[i])
            y_st[i] = a.emit(lambda e, dd=y_out[i * 128:(i + 1) * 128, :], ss=yb[:, r2 * D:(r2 + 1) * D]:
                             e.dma_start(out=dd, in_=ss), Ly[r2], 16)

    g.wait(Ly[0], y_st[6])
    g.wait(Ly[1], y_st[7])
    sy.wait(Ly[0], y_st[6])
    sy.wait(Ly[1], y_st[7])

    n_ops = {k: len(vv) for k, vv in prog.items()}
    print("[build] ops per engine:", n_ops)

    with nc.Block() as blk:
        blk.sync(lambda e: [op(e) for op in prog["sy"]])
        blk.gpsimd(lambda e: [op(e) for op in prog["g"]])
        blk.vector(lambda e: [op(e) for op in prog["v"]])
        blk.scalar(lambda e: [op(e) for op in prog["a"]])
        blk.tensor(lambda e: [op(e) for op in prog["t"]])

    return nc


_TAB_CACHE = {}


def _full_tables():
    if "t" in _TAB_CACHE:
        return _TAB_CACHE["t"]
    import jax
    import jax.numpy as jnp
    cpu = jax.devices("cpu")[0]
    with jax.default_device(cpu):
        scale_vec = (jnp.arange(0, D, 2, dtype=jnp.float32) + 0.4 * D) / (1.4 * D)
        power = jnp.arange(S, dtype=jnp.float32) / 512
        scale = scale_vec[None, :] ** power[:, None]
        scale_dn = 1.0 / scale
        inv_freq = 1.0 / 10000.0 ** (jnp.arange(1024, dtype=jnp.float32) / 1024)
        sinusoid = jnp.arange(S, dtype=jnp.float32)[:, None] * inv_freq[None, :]
        sin = jnp.sin(sinusoid)
        cos = jnp.cos(sinusoid)
        cq = cos * scale
        sq = sin * scale
        ck = cos * scale_dn
        sk = sin * scale_dn
        n = jnp.arange(S)[:, None]
        m = jnp.arange(S)[None, :]
        Dm = jnp.where(n >= m, GAMMA ** jnp.maximum(n - m, 0).astype(jnp.float32), 0.0)
        out = tuple(np.asarray(x) for x in (cq, sq, ck, sk, Dm))
    _TAB_CACHE["t"] = out
    return out


def _host_tables(own_pos, seq_pos):
    cq, sq, ck, sk, _ = _full_tables()

    def hilo(t):
        hi = np.ascontiguousarray(t.T)
        lo = np.zeros_like(hi, dtype=bfd)
        return hi, lo

    return [hilo(x) for x in (cq[own_pos], sq[own_pos], ck[seq_pos], sk[seq_pos])]


def _host_dtiles(chunk_ids):
    _, _, _, _, Dm = _full_tables()
    out = np.zeros((TOT_ITERS * CH, CH), dtype=np.float32)
    for (j, s_), t_ in ITER_IDX.items():
        gq = chunk_ids[s_]
        out[t_ * CH:(t_ + 1) * CH, :] = Dm[gq * CH:(gq + 1) * CH, j * CH:(j + 1) * CH].T
    lo = np.zeros_like(out, dtype=bfd)
    return out, lo


def make_in_maps(X, Wq, Wk, Wv, wg, wu, wd, sg, su, sd):
    wq_p = np.ascontiguousarray(np.concatenate([Wq[:, 0::2], Wq[:, 1::2]], axis=1))
    wk_p = np.ascontiguousarray(np.concatenate([Wk[:, 0::2], Wk[:, 1::2]], axis=1))
    sg_col = np.ascontiguousarray(sg.reshape(NIC, 128).T)
    su_col = np.ascontiguousarray(su.reshape(NIC, 128).T)
    sd_bc = np.ascontiguousarray(np.broadcast_to(sd[None, :], (128, D)))
    ident = np.eye(128, dtype=bfd)
    in_maps = []
    for core in range(8):
        seq = core // 4
        cg = core % 4
        chunk_ids = [cg + 4 * s_ for s_ in range(NSLOT)]
        own_pos = np.concatenate([np.arange(g_ * CH, (g_ + 1) * CH) for g_ in chunk_ids])
        x_seq_c = X[seq]
        (qch, qcl), (qsh, qsl), (kch, kcl), (ksh, ksl) = _host_tables(own_pos, np.arange(S))
        dth, dtl = _host_dtiles(chunk_ids)
        in_maps.append({
            "x_own": np.ascontiguousarray(x_seq_c[own_pos]),
            "x_seq": np.ascontiguousarray(x_seq_c),
            "wq": wq_p, "wk": wk_p, "wv": np.ascontiguousarray(Wv),
            "w_gate": wg, "w_up": wu, "w_down": wd,
            "sg_col": sg_col, "su_col": su_col, "sd_b": sd_bc,
            "ident_in": ident,
            "tqc_hi": np.ascontiguousarray(qch), "tqs_hi": np.ascontiguousarray(qsh),
            "tqc_lo": np.ascontiguousarray(qcl), "tqs_lo": np.ascontiguousarray(qsl),
            "tkc_hi": np.ascontiguousarray(kch), "tks_hi": np.ascontiguousarray(ksh),
            "tkc_lo": np.ascontiguousarray(kcl), "tks_lo": np.ascontiguousarray(ksl),
            "dt_hi": dth, "dt_lo": dtl,
        })
    return in_maps


def scatter_out(results):
    y = np.zeros((B, S, D), dtype=results[0]["y_out"].dtype)
    for core in range(8):
        seq = core // 4
        cg = core % 4
        for s_ in range(NSLOT):
            g_ = cg + 4 * s_
            y[seq, g_ * CH:(g_ + 1) * CH, :] = results[core]["y_out"][s_ * CH:(s_ + 1) * CH, :]
    return y


_NC_CACHE = {}


def kernel(L0_attn_X, L0_attn_W_Q, L0_attn_W_K, L0_attn_W_V,
           L0_ffn_w_gate, L0_ffn_w_up, L0_ffn_w_down,
           L0_ffn_s_gate, L0_ffn_s_up, L0_ffn_s_down):
    X = np.ascontiguousarray(np.asarray(L0_attn_X, dtype=np.float32))
    in_maps = make_in_maps(
        X,
        np.asarray(L0_attn_W_Q, dtype=np.float32), np.asarray(L0_attn_W_K, dtype=np.float32),
        np.asarray(L0_attn_W_V, dtype=np.float32),
        np.ascontiguousarray(np.asarray(L0_ffn_w_gate, dtype=np.int8)),
        np.ascontiguousarray(np.asarray(L0_ffn_w_up, dtype=np.int8)),
        np.ascontiguousarray(np.asarray(L0_ffn_w_down, dtype=np.int8)),
        np.asarray(L0_ffn_s_gate, dtype=np.float32), np.asarray(L0_ffn_s_up, dtype=np.float32),
        np.asarray(L0_ffn_s_down, dtype=np.float32))

    if 'nc' not in _NC_CACHE:
        _NC_CACHE['nc'] = build_nc()
    nc = _NC_CACHE['nc']
    res = run_bass_kernel_spmd(nc, in_maps, core_ids=list(range(8)))
    return scatter_out(res.results)


# version 20 (final)
# speedup vs baseline: 1.0348x; 1.0348x over previous
import sys
sys.path.insert(0, '/opt/trn_rl_repo')
import math
import numpy as np
import ml_dtypes
import concourse.bass as bass
import concourse.mybir as mybir
from concourse.bass_utils import run_bass_kernel_spmd

F32 = mybir.dt.float32
BF16 = mybir.dt.bfloat16
I8 = mybir.dt.int8
AF = mybir.ActivationFunctionType
ALU = mybir.AluOpType
bfd = ml_dtypes.bfloat16

D = 2048
NCH = 16
PCH = 8
S = 4096
B = 2
NOWN = 1024
CH = 256
CAPS = [4, 8, 12, 16]
NSLOT = 4
TOT_ITERS = sum(CAPS)
INTER = 8192
NIC = 64
GAMMA = 0.96875
EPS = 1e-5


def _valid_slots(j):
    return [s for s in range(NSLOT) if CAPS[s] > j]


ITER_IDX = {}
_t = 0
for _j in range(16):
    for _s in _valid_slots(_j):
        ITER_IDX[(_j, _s)] = _t
        _t += 1


class Sem:
    def __init__(self, nc, name):
        self.h = nc.alloc_semaphore(name)
        self.n = 0
        self.name = name


class EQ:
    def __init__(self, prog, key):
        self.prog = prog
        self.key = key

    def wait(self, sem, val):
        if val is None or val <= 0:
            return
        self.prog[self.key].append(lambda e, s=sem.h, v=val: e.wait_ge(s, v))

    def emit(self, fn, sem=None, k=1):
        if sem is None:
            self.prog[self.key].append(lambda e, f=fn: f(e))
            return None
        sem.n += k
        tgt = sem.n

        def run(e, f=fn, s=sem.h, kk=k):
            f(e).then_inc(s, kk)

        self.prog[self.key].append(run)
        return tgt


def build_nc(debug_taps=False):
    nc = bass.Bass("TRN2", target_bir_lowering=False, debug=False)

    x_own = nc.dram_tensor("x_own", [NOWN, D], F32, kind="ExternalInput").ap()
    x_seq = nc.dram_tensor("x_seq", [S, D], F32, kind="ExternalInput").ap()
    wq_i = nc.dram_tensor("wq", [D, D], F32, kind="ExternalInput").ap()
    wk_i = nc.dram_tensor("wk", [D, D], F32, kind="ExternalInput").ap()
    wv_i = nc.dram_tensor("wv", [D, D], F32, kind="ExternalInput").ap()
    w_gate = nc.dram_tensor("w_gate", [D, INTER], I8, kind="ExternalInput").ap()
    w_up = nc.dram_tensor("w_up", [D, INTER], I8, kind="ExternalInput").ap()
    w_down = nc.dram_tensor("w_down", [INTER, D], I8, kind="ExternalInput").ap()
    sg_col = nc.dram_tensor("sg_col", [128, NIC], F32, kind="ExternalInput").ap()
    su_col = nc.dram_tensor("su_col", [128, NIC], F32, kind="ExternalInput").ap()
    sd_b = nc.dram_tensor("sd_b", [128, D], F32, kind="ExternalInput").ap()
    ident_in = nc.dram_tensor("ident_in", [128, 128], BF16, kind="ExternalInput").ap()
    tqc_hi = nc.dram_tensor("tqc_hi", [1024, NOWN], F32, kind="ExternalInput").ap()
    tqs_hi = nc.dram_tensor("tqs_hi", [1024, NOWN], F32, kind="ExternalInput").ap()
    tqc_lo = nc.dram_tensor("tqc_lo", [1024, NOWN], BF16, kind="ExternalInput").ap()
    tqs_lo = nc.dram_tensor("tqs_lo", [1024, NOWN], BF16, kind="ExternalInput").ap()
    tkc_hi = nc.dram_tensor("tkc_hi", [1024, S], F32, kind="ExternalInput").ap()
    tks_hi = nc.dram_tensor("tks_hi", [1024, S], F32, kind="ExternalInput").ap()
    tkc_lo = nc.dram_tensor("tkc_lo", [1024, S], BF16, kind="ExternalInput").ap()
    tks_lo = nc.dram_tensor("tks_lo", [1024, S], BF16, kind="ExternalInput").ap()
    dt_hi = nc.dram_tensor("dt_hi", [TOT_ITERS * CH, CH], F32, kind="ExternalInput").ap()
    dt_lo = nc.dram_tensor("dt_lo", [TOT_ITERS * CH, CH], BF16, kind="ExternalInput").ap()

    y_out = nc.dram_tensor("y_out", [NOWN, D], F32, kind="ExternalOutput").ap()

    dkind = "ExternalOutput" if debug_taps else "Internal"
    xlnT_seq_d = nc.dram_tensor("xlnT_seq_d", [D, S], BF16, kind=dkind).ap()
    xlnT_own_d = nc.dram_tensor("xlnT_own_d", [D, NOWN], BF16, kind=dkind).ap()
    qt_d = nc.dram_tensor("qt_d", [D, NOWN], BF16, kind=dkind).ap()
    kt_d = nc.dram_tensor("kt_d", [D, S], BF16, kind=dkind).ap()
    v_d = nc.dram_tensor("v_d", [S, D], BF16, kind=dkind).ap()
    x2_d = nc.dram_tensor("x2_d", [NOWN, D], F32, kind=dkind).ap()
    x2lnT_d = nc.dram_tensor("x2lnT_d", [D, NOWN], BF16, kind=dkind).ap()
    if debug_taps:
        av_tap = nc.dram_tensor("av_tap", [8 * 128, D], F32, kind="ExternalOutput").ap()

    SP = Sem(nc, "SP")
    SV = Sem(nc, "SV")
    SA = Sem(nc, "SA")
    sems_all = [SP, SV, SA]

    def lanes(name, n):
        out = []
        for i in range(n):
            s_ = Sem(nc, f"{name}{i}")
            sems_all.append(s_)
            out.append(s_)
        return out

    ident = nc.alloc_sbuf_tensor("ident", [128, 128], BF16, side="right")
    sgc = nc.alloc_sbuf_tensor("sgc", [128, NIC], F32, side="right")
    suc = nc.alloc_sbuf_tensor("suc", [128, NIC], F32, side="right")
    sdb = nc.alloc_sbuf_tensor("sdb", [128, D], F32, side="right")
    epsb = nc.alloc_sbuf_tensor("epsb", [128, 1], F32, side="right")
    NR = 4
    st6 = nc.alloc_sbuf_tensor("st6", [128, 24 * NR], F32, side="right")
    ag2 = nc.alloc_sbuf_tensor("ag2", [128, 2 * NR], F32, side="right")
    sdv_ = nc.alloc_sbuf_tensor("sdv_", [128, NR], F32, side="right")
    rsd = nc.alloc_sbuf_tensor("rsd", [128, NR], F32, side="right")
    xs1 = nc.alloc_sbuf_tensor("xs1", [128, 512], F32, side="right")
    xs2 = nc.alloc_sbuf_tensor("xs2", [128, 512], F32, side="right")

    pb = {i: nc.alloc_psum_tensor(f"pb{i}", [128, 512], F32) for i in range(4)}
    pb[4] = None
    pb[5] = None
    pb[6] = None
    pb[7] = None
    pscopes = []

    def palloc(name, shape, dtype):
        cm = nc.psum_tensor(name, shape, dtype)
        h = cm.__enter__()
        pscopes.append((name, cm))
        return h

    def pfree(*names):
        for name in names:
            for i2, (n2, cm) in enumerate(pscopes):
                if n2 == name:
                    cm.__exit__(None, None, None)
                    pscopes.pop(i2)
                    break

    tpx = [None, None, None, None]

    prog = {k: [] for k in ("sy", "g", "v", "a", "t")}
    sy, g, v, a, t = (EQ(prog, k) for k in ("sy", "g", "v", "a", "t"))
    all_eq = [sy, g, v, a, t]

    bank_free = {b: None for b in range(8)}

    def bank_wait(eq, b):
        if bank_free[b] is not None:
            sem, tgt = bank_free[b]
            eq.wait(sem, tgt)

    _fence_snap = {}

    def fence():
        for eq in all_eq:
            for s_ in sems_all:
                if _fence_snap.get(s_.name) != s_.n:
                    eq.wait(s_, s_.n)
        for s_ in sems_all:
            _fence_snap[s_.name] = s_.n

    scopes = []

    def salloc(name, shape, dtype):
        cm = nc.sbuf_tensor(name, shape, dtype)
        h = cm.__enter__()
        scopes.append((name, cm))
        return h

    def sfree(*names):
        for name in names:
            for i, (n_, cm) in enumerate(scopes):
                if n_ == name:
                    cm.__exit__(None, None, None)
                    scopes.pop(i)
                    break

    slot_war = {r: 0 for r in range(NR)}

    def ln_stats(src_ap, r, pre_waits=()):
        v.wait(SV, slot_war[r])
        for (sem, tgt) in pre_waits:
            v.wait(sem, tgt)
        nchunk = src_ap.shape[-1] // 512
        for c_ in range(nchunk):
            v.emit(lambda e, s_=src_ap[:, c_ * 512:(c_ + 1) * 512], rr=r, c2=c_:
                   e.bn_stats(st6[:, 24 * rr + 6 * c2:24 * rr + 6 * c2 + 6], s_), SV, 1)
        v.wait(SV, SV.n)
        t1 = v.emit(lambda e, rr=r, n_=nchunk: e.bn_aggr(ag2[:, 2 * rr:2 * rr + 2],
                                                         st6[:, 24 * rr:24 * rr + 6 * n_]), SV, 1)
        a.wait(SV, t1)
        t2 = a.emit(lambda e, rr=r: e.activation(sdv_[:, rr:rr + 1], ag2[:, 2 * rr + 1:2 * rr + 2],
                                                 AF.Sqrt, bias=epsb[:]), SA, 1)
        v.wait(SA, t2)
        t3 = v.emit(lambda e, rr=r: e.reciprocal(rsd[:, rr:rr + 1], sdv_[:, rr:rr + 1]), SV, 1)
        return t3

    def ln_norm(dst_ap, src_ap, r, rec):
        v.wait(SV, rec)
        tgt = v.emit(lambda e, dd=dst_ap, s_=src_ap, rr=r:
                     e.tensor_scalar(dd, s_, ag2[:, 2 * rr:2 * rr + 1], rsd[:, rr:rr + 1],
                                     op0=ALU.subtract, op1=ALU.mult), SV, 1)
        slot_war[r] = tgt
        return tgt

    Linit = lanes("Li", 1)[0]
    sy.emit(lambda e: e.dma_start(out=ident[:], in_=ident_in), Linit, 16)
    sy.emit(lambda e: e.dma_start(out=sgc[:], in_=sg_col), Linit, 16)
    sy.emit(lambda e: e.dma_start(out=suc[:], in_=su_col), Linit, 16)
    sy.emit(lambda e: e.dma_start(out=sdb[:], in_=sd_b), Linit, 16)
    init_q = Linit.n
    eps_hi = float(np.float32(EPS))
    eps_lo = float(np.float32(EPS - eps_hi))
    v.emit(lambda e: e.memset(epsb[:], eps_hi), SV, 1)
    v.emit(lambda e: e.memset(xs1[:, 0:1], eps_lo), SV, 1)
    v.wait(SV, SV.n)
    eps_t = v.emit(lambda e: e.tensor_tensor(epsb[:], epsb[:], xs1[:, 0:1], op=ALU.add), SV, 1)
    slot_war[0] = eps_t

    W_sb = salloc("W_sb", [128, NCH * D], BF16)
    wv_ap = W_sb.ap().rearrange("p (c o) -> p c o", c=NCH)
    LW = lanes("LW", 4)
    w_loaded = {}

    def load_w(wap, key):
        qs_ = []
        for q4 in range(4):
            qs_.append(g.emit(lambda e, w=wap, q5=q4: e.dma_start(
                out=W_sb.ap().rearrange("p (c o) -> p c o", c=NCH)[:, q5 * 4:(q5 + 1) * 4],
                in_=w.rearrange("(c p) o -> p c o", p=128)[:, q5 * 4:(q5 + 1) * 4]), LW[q4], 16))
        w_loaded[key] = qs_

    load_w(wq_i, "q")

    xst = salloc("xst", [128, 4 * D], F32)
    xlnb = salloc("xlnb", [128, 4 * D], BF16)
    xtg = salloc("xtg", [128, 2 * NCH * 512], BF16)
    xtgv = xtg.ap().rearrange("p (gg c t) -> p gg c t", gg=2, c=NCH)
    Lx = lanes("Lx", 4)
    Lxt = lanes("Lxt", 2)
    for q_ in range(4):
        tpx[q_] = palloc(f"tp{q_}", [128, 1024], BF16)

    tiles = [(x_seq, i) for i in range(32)] + [(x_own, i) for i in range(8)]
    ld_t, ts_done, tp_half, ev_done, stq = {}, {}, {}, {}, {}
    aggr_t, sqrt_t = {}, {}
    t.wait(Linit, init_q)

    def p1_stage_a(i):
        xsrc, ti = tiles[i]
        sl = xst[:, (i % 4) * D:(i % 4 + 1) * D]
        if i >= 4:
            sy.wait(SV, ts_done[i - 4])
        ld_t[i] = sy.emit(lambda e, dd=sl, ss=xsrc[ti * 128:(ti + 1) * 128, :]:
                          e.dma_start(out=dd, in_=ss), Lx[i % 4], 16)
        r = i % NR
        v.wait(SV, slot_war[r])
        v.wait(Lx[i % 4], ld_t[i])
        for c_ in range(4):
            v.emit(lambda e, s_=sl[:, c_ * 512:(c_ + 1) * 512], rr=r, c2=c_:
                   e.bn_stats(st6[:, 24 * rr + 6 * c2:24 * rr + 6 * c2 + 6], s_), SV, 1)
        v.wait(SV, SV.n)
        aggr_t[i] = v.emit(lambda e, rr=r: e.bn_aggr(ag2[:, 2 * rr:2 * rr + 2],
                                                     st6[:, 24 * rr:24 * rr + 24]), SV, 1)
        a.wait(SV, aggr_t[i])
        sqrt_t[i] = a.emit(lambda e, rr=r: e.activation(sdv_[:, rr:rr + 1], ag2[:, 2 * rr + 1:2 * rr + 2],
                                                        AF.Sqrt, bias=epsb[:]), SA, 1)

    p1_stage_a(0)
    for i, (xsrc, ti) in enumerate(tiles):
        if i + 1 < len(tiles):
            p1_stage_a(i + 1)
        sl = xst[:, (i % 4) * D:(i % 4 + 1) * D]
        r = i % NR
        v.wait(SA, sqrt_t[i])
        rec = v.emit(lambda e, rr=r: e.reciprocal(rsd[:, rr:rr + 1], sdv_[:, rr:rr + 1]), SV, 1)
        xl = xlnb[:, (i % 4) * D:(i % 4 + 1) * D]
        if i >= 4:
            v.wait(SP, tp_half[(i - 4, 1)])
        ts_done[i] = ln_norm(xl, sl, r, rec)
        gslot = (i // 4) % 2
        tg = i % 4
        t.wait(SV, ts_done[i])
        if tg == 0 and i // 4 >= 2:
            t.wait(Lxt[gslot], stq[i // 4 - 2])
        for half in range(2):
            tb_ = (i * 2 + half) % 4
            bank_wait(t, 4 + tb_)
            for c in range(8):
                ch = half * 8 + c
                t.emit(lambda e, ph=tpx[tb_], cc=c, ss=xl[:, ch * 128:(ch + 1) * 128]:
                       e.transpose(ph[:, cc * 128:(cc + 1) * 128], ss, ident[:]),
                       SP if c == 7 else None, 1)
            tp_half[(i, half)] = SP.n
            a.wait(SP, tp_half[(i, half)])
            dst = xtgv[:, gslot, half * 8:half * 8 + 8, tg * 128:tg * 128 + 128]
            evt = a.emit(lambda e, dd=dst, ph=tpx[tb_]:
                         e.activation(dd, ph[:, 0:1024].rearrange("p (c t) -> p c t", c=8), AF.Copy),
                         SA, 1)
            bank_free[4 + tb_] = (SA, evt)
        ev_done[i] = SA.n
        if tg == 3:
            gidx = i // 4
            if gidx < 8:
                dstap = xlnT_seq_d.rearrange("(c p) t -> p c t", p=128)[:, :, gidx * 512:(gidx + 1) * 512]
            else:
                dstap = xlnT_own_d.rearrange("(c p) t -> p c t", p=128)[:, :, (gidx - 8) * 512:(gidx - 7) * 512]
            a.wait(SA, ev_done[i])
            stq[gidx] = a.emit(lambda e, dd=dstap, ss=xtgv[:, gslot]: e.dma_start(out=dd, in_=ss),
                               Lxt[gslot], 16)

    fence()
    sfree("xtg", "xlnb", "xst")
    pfree("tp3", "tp2", "tp1", "tp0")
    for q_ in range(4, 8):
        pb[q_] = palloc(f"pf{q_}", [128, 512], F32)

    xT = salloc("xT", [128, 2 * NCH * 512], BF16)
    tbuf = salloc("tbuf", [128, 2 * 2 * PCH * 512], F32)
    qst = salloc("qst", [128, 2 * 2 * 512], BF16)

    xtv_all = xT.ap().rearrange("p (r c t) -> p r c t", r=2, c=NCH)
    tbv = tbuf.ap().rearrange("p (r k c t) -> p r k c t", r=2, k=2, c=PCH)
    qsv = qst.ap().rearrange("p (r u t) -> p r u t", r=2, u=2)
    LxT = lanes("LxT", 2)
    Ltb = lanes("Ltb", 2)
    Lqs = lanes("Lqs", 2)
    qst_last = {0: None, 1: None}
    tb_last = {0: None, 1: None, 2: None}
    xT_last = {0: None, 1: None}

    def proj_qk(n_groups, xlnT_src, tabs, out_d, wkey):
        tc_hi, ts_hi, tc_lo, ts_lo = tabs
        xt_t, tb_t, st_t = {}, {}, {}
        pe_grp, dv_grp = {}, {}
        for gi in range(n_groups):
            rs = gi % 2
            if xT_last[rs] is not None:
                sy.wait(SP, xT_last[rs])
            xt_t[gi] = sy.emit(lambda e, r=rs,
                               ss=xlnT_src.rearrange("(c p) t -> p c t", p=128)[:, :, gi * 512:(gi + 1) * 512]:
                               e.dma_start(out=xtv_all[:, r], in_=ss), LxT[rs], 16)
            rt = gi % 2
            if tb_last[rt] is not None:
                g.wait(SV, tb_last[rt])
            hi_ts = []
            for kk in range(2):
                thi = tc_hi if kk == 0 else ts_hi
                hi_ts.append(g.emit(lambda e, r=rt, k2=kk,
                                    ss=thi.rearrange("(c p) t -> p c t", p=128)[:, :, gi * 512:(gi + 1) * 512]:
                                    e.dma_start(out=tbv[:, r, k2], in_=ss), Ltb[rt], 16))
            g.wait(Ltb[rt], hi_ts[1])
            for kk in range(2):
                tlo = tc_lo if kk == 0 else ts_lo
                tb_t[gi] = g.emit(lambda e, r=rt, k2=kk,
                                  ss=tlo.rearrange("(c p) t -> p c t", p=128)[:, :, gi * 512:(gi + 1) * 512]:
                                  e.dma_start(out=tbv[:, r, k2], in_=ss, accum_op=ALU.add), Ltb[rt], 16)
            if gi == 0:
                pass
            else:
                t.wait(LW[3], w_loaded[wkey][3])
            t.wait(LxT[rs], xt_t[gi])
            for p_ in range(8):
                be, bo = 2 * (p_ % 3), 2 * (p_ % 3) + 1
                qr = p_ % 2
                bank_wait(t, be)
                bank_wait(t, bo)
                for c in range(NCH):
                    if gi == 0 and p_ == 0 and c % 4 == 0:
                        t.wait(LW[c // 4], w_loaded[wkey][c // 4])
                    t.emit(lambda e, b=be, cc=c, pp=p_, r=rs:
                           e.matmul(pb[b][:], wv_ap[:, cc, pp * 128:(pp + 1) * 128], xtv_all[:, r, cc],
                                    start=(cc == 0), stop=(cc == NCH - 1)))
                for c in range(NCH):
                    t.emit(lambda e, b=bo, cc=c, pp=p_, r=rs:
                           e.matmul(pb[b][:], wv_ap[:, cc, (pp + 8) * 128:(pp + 9) * 128], xtv_all[:, r, cc],
                                    start=(cc == 0), stop=(cc == NCH - 1)),
                           SP if c == NCH - 1 else None, 1)
                pe_t = SP.n
                v.wait(SP, pe_t)
                v.wait(Ltb[rt], tb_t[gi])
                if qst_last[qr] is not None:
                    v.wait(Lqs[qr], qst_last[qr])
                v.wait(SV, SV.n)
                v.emit(lambda e, b=be, r=rt, pp=p_: e.tensor_tensor(xs1[:], pb[b][:], tbv[:, r, 0, pp], op=ALU.mult), SV, 1)
                v.emit(lambda e, b=bo, r=rt, pp=p_: e.tensor_tensor(xs2[:], pb[b][:], tbv[:, r, 1, pp], op=ALU.mult), SV, 1)
                v.wait(SV, SV.n)
                v.emit(lambda e, q2=qr: e.tensor_tensor(qsv[:, q2, 0], xs1[:], xs2[:], op=ALU.subtract), SV, 1)
                v.wait(SV, SV.n)
                v.emit(lambda e, b=bo, r=rt, pp=p_: e.tensor_tensor(xs1[:], pb[b][:], tbv[:, r, 0, pp], op=ALU.mult), SV, 1)
                v.emit(lambda e, b=be, r=rt, pp=p_: e.tensor_tensor(xs2[:], pb[b][:], tbv[:, r, 1, pp], op=ALU.mult), SV, 1)
                v.wait(SV, SV.n)
                dvp = v.emit(lambda e, q2=qr: e.tensor_tensor(qsv[:, q2, 1], xs1[:], xs2[:], op=ALU.add), SV, 1)
                bank_free[be] = (SV, dvp)
                bank_free[bo] = (SV, dvp)
                a.wait(SV, dvp)
                dpair = out_d.rearrange("(h f p) t -> p h f t", p=128, h=2)[:, :, p_, gi * 512:(gi + 1) * 512]
                st_t[(gi, p_)] = a.emit(lambda e, q2=qr, dd=dpair:
                                        e.dma_start(out=dd, in_=qsv[:, q2]), Lqs[qr], 16)
                qst_last[qr] = st_t[(gi, p_)]
            dv_grp[gi] = SV.n
            pe_grp[gi] = SP.n
            tb_last[rt] = SV.n
            xT_last[rs] = SP.n
        return st_t

    proj_qk(2, xlnT_own_d, (tqc_hi, tqs_hi, tqc_lo, tqs_lo), qt_d, "q")
    g.wait(SP, SP.n)
    load_w(wk_i, "k")
    proj_qk(8, xlnT_seq_d, (tkc_hi, tks_hi, tkc_lo, tks_lo), kt_d, "k")
    g.wait(SP, SP.n)
    load_w(wv_i, "v")

    fence()
    sfree("qst", "tbuf")
    vst = salloc("vst", [128, 3 * 4 * D], BF16)
    vsv = vst.ap().rearrange("p (r u d) -> p r u d", r=3, u=4)
    Lvs = lanes("Lvs", 3)
    v_xt, v_st, v_pe_grp = {}, {}, {}
    for gi in range(8):
        rs = gi % 2
        if xT_last[rs] is not None:
            sy.wait(SP, xT_last[rs])
        v_xt[gi] = sy.emit(lambda e, r=rs,
                           ss=xlnT_seq_d.rearrange("(c p) t -> p c t", p=128)[:, :, gi * 512:(gi + 1) * 512]:
                           e.dma_start(out=xtv_all[:, r], in_=ss), LxT[rs], 16)
        if gi > 0:
            t.wait(LW[3], w_loaded["v"][3])
        t.wait(LxT[rs], v_xt[gi])
        first_ev = True
        for ts_ in range(4):
            for j in range(4):
                bk = (ts_ * 4 + j) % 6
                bank_wait(t, bk)
                for c in range(NCH):
                    if gi == 0 and ts_ == 0 and j == 0 and c % 4 == 0:
                        t.wait(LW[c // 4], w_loaded["v"][c // 4])
                    t.emit(lambda e, b=bk, cc=c, tt=ts_, jj=j, r=rs:
                           e.matmul(pb[b][:], xtv_all[:, r, cc, tt * 128:(tt + 1) * 128],
                                    wv_ap[:, cc, jj * 512:(jj + 1) * 512],
                                    start=(cc == 0), stop=(cc == NCH - 1)),
                           SP if c == NCH - 1 else None, 1)
                a.wait(SP, SP.n)
                if first_ev and gi >= 3:
                    a.wait(Lvs[gi % 3], v_st[gi - 3])
                first_ev = False
                evt = a.emit(lambda e, b=bk, tt=ts_, jj=j, r=gi % 3:
                             e.activation(vsv[:, r, tt, jj * 512:(jj + 1) * 512], pb[b][:], AF.Copy), SA, 1)
                bank_free[bk] = (SA, evt)
        v_pe_grp[gi] = SP.n
        xT_last[rs] = SP.n
        a.wait(SA, SA.n)
        v_st[gi] = a.emit(lambda e, r=gi % 3, dd=v_d.rearrange("(u p) d -> p u d", p=128)[:, gi * 4:(gi + 1) * 4, :]:
                          e.dma_start(out=dd, in_=vsv[:, r]), Lvs[gi % 3], 16)

    sy.wait(SP, SP.n)
    g.wait(SP, SP.n)
    v.wait(SP, SP.n)
    for lv in list(Lvs) + list(Lqs):
        sy.wait(lv, lv.n)
        g.wait(lv, lv.n)
    sfree("vst", "xT", "W_sb")

    att_sb = salloc("att_sb", [128, TOT_ITERS * 2 * CH], BF16)
    vb = salloc("vb", [128, 4 * 4 * 512], BF16)
    qT_sb = salloc("qT_sb", [128, NSLOT * NCH * CH], BF16)
    ktT = salloc("ktT", [128, 2 * NCH * CH], BF16)
    Dcb = salloc("Dcb", [128, 4 * 4 * 2 * CH], F32)
    ktv_all = ktT.ap().rearrange("p (r c t) -> p r c t", r=2, c=NCH)
    dcv = Dcb.ap().rearrange("p (r n u q) -> p r n u q", r=4, n=4, u=2)
    qv_all = qT_sb.ap().rearrange("p (c s t) -> p c s t", c=NCH, s=NSLOT)
    attv = att_sb.ap().rearrange("p (t u q) -> p t u q", t=TOT_ITERS, u=2)
    Lq = lanes("Lq", 1)[0]
    Lkt = lanes("Lkt", 2)
    Ldc = lanes("Ldc", 4)

    for s_ in range(NSLOT):
        sy.emit(lambda e, ss2=s_,
                ss=qt_d.rearrange("(c p) t -> p c t", p=128)[:, :, s_ * CH:(s_ + 1) * CH]:
                e.dma_start(out=qv_all[:, :, ss2], in_=ss), Lq, 16)
    qld_all = Lq.n
    a3_pe, a3_dv, kt_t, dc_t = {}, {}, {}, {}
    for j in range(16):
        rs = j % 2
        vs = _valid_slots(j)
        if j >= 2:
            sy.wait(SP, a3_pe[j - 2])
        kt_t[j] = sy.emit(lambda e, r=rs,
                          ss=kt_d.rearrange("(c p) t -> p c t", p=128)[:, :, j * CH:(j + 1) * CH]:
                          e.dma_start(out=ktv_all[:, r], in_=ss), Lkt[rs], 16)
        off = ITER_IDX[(j, vs[0])]
        nv = len(vs)
        rd = j % 4
        if j >= 4:
            g.wait(SV, a3_dv[j - 4])
        hi_t = g.emit(lambda e, r=rd, n_=nv,
                      ss=dt_hi[off * CH:(off + nv) * CH, :].rearrange("(n u p) q -> p n u q", p=128, u=2):
                      e.dma_start(out=dcv[:, r, 0:n_], in_=ss), Ldc[rd], 16)
        g.wait(Ldc[rd], hi_t)
        dc_t[j] = g.emit(lambda e, r=rd, n_=nv,
                         ss=dt_lo[off * CH:(off + nv) * CH, :].rearrange("(n u p) q -> p n u q", p=128, u=2):
                         e.dma_start(out=dcv[:, r, 0:n_], in_=ss, accum_op=ALU.add), Ldc[rd], 16)
        t.wait(Lkt[rs], kt_t[j])
        if j == 0:
            t.wait(Lq, qld_all)
        pairs = [vs[i2:i2 + 2] for i2 in range(0, len(vs), 2)]
        bkc = [0]
        for pr in pairs:
            s0 = pr[0]
            i0 = vs.index(s0)
            w_ = len(pr)
            for u in range(2):
                bk = bkc[0] % 6
                bkc[0] += 1
                bank_wait(t, bk)
                for c in range(NCH):
                    t.emit(lambda e, b=bk, cc=c, r=rs, uu=u, ss2=s0, w2=w_:
                           e.matmul(pb[b][:, 0:w2 * CH], ktv_all[:, r, cc, uu * 128:(uu + 1) * 128],
                                    qv_all[:, cc, ss2:ss2 + w2].rearrange("p s t -> p (s t)"),
                                    start=(cc == 0), stop=(cc == NCH - 1)),
                           SP if c == NCH - 1 else None, 1)
                v.wait(SP, SP.n)
                v.wait(Ldc[rd], dc_t[j])
                evt = None
                for w2 in range(w_):
                    evt = v.emit(lambda e, b=bk, r=rd, n_=i0 + w2, uu=u, tx=off + i0 + w2, o2=w2:
                                 e.tensor_tensor(attv[:, tx, uu], pb[b][:, o2 * CH:(o2 + 1) * CH],
                                                 dcv[:, r, n_, uu], op=ALU.mult), SV, 1)
                bank_free[bk] = (SV, evt)
        a3_pe[j] = SP.n
        a3_dv[j] = SV.n

    a3_final_dv = SV.n
    sfree("Dcb", "ktT", "qT_sb")

    av_sb = salloc("av_sb", [128, 8 * D], F32)
    vbv = vb.ap().rearrange("p (r u d) -> p r u d", r=4, u=4)
    avv = av_sb.ap().rearrange("p (i d) -> p i d", i=8)
    Lvb = lanes("Lvb", 4)
    b_ld, b_pe, b_ac = {}, {}, {}
    for dq in range(4):
        for jp in range(8):
            slot = jp % 4
            idx = dq * 8 + jp
            if idx >= 4:
                sy.wait(SP, b_pe[idx - 4])
            b_ld[idx] = sy.emit(lambda e, sl=slot,
                                ss=v_d.rearrange("(u p) d -> p u d", p=128)[:, jp * 4:(jp + 1) * 4, dq * 512:(dq + 1) * 512]:
                                e.dma_start(out=vbv[:, sl], in_=ss), Lvb[slot], 16)
            t.wait(Lvb[slot], b_ld[idx])
            if dq == 0 and jp == 0:
                t.wait(SV, a3_final_dv)
                for b_ in range(8):
                    bank_wait(t, b_)
            if dq > 0 and jp == 0:
                t.wait(SA, b_ac[dq - 1])
            lastj = 2 * jp + 1
            for j in (2 * jp, 2 * jp + 1):
                for s_ in _valid_slots(j):
                    tix = ITER_IDX[(j, s_)]
                    for u in range(2):
                        for qs in range(2):
                            last = (j == lastj and s_ == _valid_slots(lastj)[-1] and u == 1 and qs == 1)
                            t.emit(lambda e, ph=pb[s_ * 2 + qs], tx=tix, uu=u, q2=qs, sl=slot, jj=j, ss2=s_:
                                   e.matmul(ph[:], attv[:, tx, uu, q2 * 128:(q2 + 1) * 128],
                                            vbv[:, sl, (jj % 2) * 2 + uu],
                                            start=(jj == 0 and uu == 0), stop=(jj == CAPS[ss2] - 1 and uu == 1)),
                                   SP if last else None, 1)
            b_pe[idx] = SP.n
        a.wait(SP, b_pe[dq * 8 + 7])
        for i8_ in range(8):
            a.emit(lambda e, ii=i8_, qq=dq, ph=pb[i8_]: e.activation(avv[:, ii, qq * 512:(qq + 1) * 512], ph[:], AF.Copy),
                   SA, 1)
        b_ac[dq] = SA.n
    for b_ in range(8):
        bank_free[b_] = (SA, b_ac[3])
    if debug_taps:
        Ltap = lanes("Ltap", 1)[0]
        sy.wait(SA, b_ac[3])
        sy.emit(lambda e: e.dma_start(out=av_tap.rearrange("(i p) d -> p i d", p=128), in_=avv[:, :, :]), Ltap, 16)

    sy.wait(SP, SP.n)
    v.wait(SP, SP.n)
    a.wait(SP, SP.n)
    pfree("pf7", "pf6", "pf5", "pf4")
    for q_ in range(4):
        tpx[q_] = palloc(f"tq{q_}", [128, 1024], BF16)

    xol = salloc("xol", [128, 2 * D], F32)
    x2b = salloc("x2b", [128, 2 * D], F32)
    x2l = salloc("x2l", [128, 2 * D], BF16)
    x2s = salloc("x2s", [128, 2 * NCH * 128], BF16)
    x2sv = x2s.ap().rearrange("p (r c t) -> p r c t", r=2, c=NCH)
    Lxo = lanes("Lxo", 2)
    Lx2 = lanes("Lx2", 2)
    Lfm = lanes("Lfm", 2)
    p5_dv, p5_pe, p5_st, p5_ac, p5_fmst = {}, {}, {}, {}, {}
    for i in range(8):
        r2 = i % 2
        r = i % NR
        if i >= 2:
            sy.wait(SV, p5_dv[i - 2])
        ld = sy.emit(lambda e, dd=xol[:, r2 * D:(r2 + 1) * D], ss=x_own[i * 128:(i + 1) * 128, :]:
                     e.dma_start(out=dd, in_=ss), Lxo[r2], 16)
        avt = avv[:, i, :]
        rec = ln_stats(avt, r, pre_waits=[(SA, b_ac[3])])
        if i >= 2:
            v.wait(Lx2[r2], p5_st[i - 2])
        t_ln2 = ln_norm(x2b[:, r2 * D:(r2 + 1) * D], avt, r, rec)
        v.wait(Lxo[r2], ld)
        v.wait(SV, t_ln2)
        t_x2 = v.emit(lambda e, dd=x2b[:, r2 * D:(r2 + 1) * D],
                      s2=xol[:, r2 * D:(r2 + 1) * D]: e.tensor_tensor(dd, dd, s2, op=ALU.add), SV, 1)
        g.wait(SV, t_x2)
        p5_st[i] = g.emit(lambda e, dd=x2_d[i * 128:(i + 1) * 128, :], ss=x2b[:, r2 * D:(r2 + 1) * D]:
                          e.dma_start(out=dd, in_=ss), Lx2[r2], 16)
        rec3 = ln_stats(x2b[:, r2 * D:(r2 + 1) * D], r, pre_waits=[(SV, t_x2)])
        if i >= 2:
            v.wait(SP, p5_pe[i - 2])
        p5_dv[i] = ln_norm(x2l[:, r2 * D:(r2 + 1) * D], x2b[:, r2 * D:(r2 + 1) * D], r, rec3)
        t.wait(SV, p5_dv[i])
        if i >= 2:
            a.wait(Lfm[r2], p5_fmst[i - 2])
        for half in range(2):
            tb_ = (i * 2 + half) % 4
            bank_wait(t, 4 + tb_)
            for c in range(8):
                ch = half * 8 + c
                t.emit(lambda e, ph=tpx[tb_], cc=c, ss=x2l[:, r2 * D + ch * 128:r2 * D + (ch + 1) * 128]:
                       e.transpose(ph[:, cc * 128:(cc + 1) * 128], ss, ident[:]),
                       SP if c == 7 else None, 1)
            php = SP.n
            a.wait(SP, php)
            dst = x2sv[:, r2, half * 8:half * 8 + 8, :]
            evt = a.emit(lambda e, dd=dst, ph=tpx[tb_]:
                         e.activation(dd, ph[:, 0:1024].rearrange("p (c t) -> p c t", c=8), AF.Copy), SA, 1)
            bank_free[4 + tb_] = (SA, evt)
        p5_pe[i] = SP.n
        p5_ac[i] = SA.n
        g.wait(SA, p5_ac[i])
        p5_fmst[i] = g.emit(lambda e, dd=x2lnT_d.rearrange("(c p) t -> p c t", p=128)[:, :, i * 128:(i + 1) * 128],
                            ss=x2sv[:, r2]: e.dma_start(out=dd, in_=ss), Lfm[r2], 16)

    fence()
    sfree("x2s", "x2l", "x2b", "xol", "av_sb", "vb", "att_sb")
    pfree("tq3", "tq2", "tq1", "tq0")
    for q_ in range(4, 8):
        pb[q_] = palloc(f"pg{q_}", [128, 512], F32)

    GI = 256
    fuse = salloc("fuse", [128, NIC * 512], BF16)
    wgu = salloc("wgu", [128, 2 * 2 * NCH * GI], BF16)
    wdb = salloc("wdb", [128, 2 * 2 * 2 * 512], BF16)
    x2T = salloc("x2T", [128, NCH * 512], BF16)
    gsb = salloc("gsb", [128, 2 * 512], F32)
    usb = salloc("usb", [128, 2 * 512], F32)
    ffb = salloc("ffb", [128, 4 * D], F32)
    x2r = salloc("x2r", [128, 2 * D], F32)
    yb = salloc("yb", [128, 2 * D], F32)

    wguv = wgu.ap().rearrange("p (w r c i) -> p w r c i", w=2, r=2, c=NCH)
    wdv = wdb.ap().rearrange("p (r n u d) -> p r n u d", r=2, n=2, u=2)
    fusev = fuse.ap().rearrange("p (i t) -> p i t", i=NIC)
    ffv = ffb.ap().rearrange("p (u d) -> p u d", u=4)
    x2Tv = x2T.ap().rearrange("p (c t) -> p c t", c=NCH)
    Lgu = lanes("Lgu", 2)
    Lwd = lanes("Lwd", 2)
    Lxx = lanes("Lxx", 1)[0]
    Lxr = lanes("Lxr", 2)
    Ly = lanes("Ly", 2)

    f_w, f_dv, f_pe, f_ev, y_st, ydv = {}, {}, {}, {}, {}, {}
    x2T_t = {}
    for cki in range(2):
        if cki >= 1:
            sy.wait(SP, f_pe[("f1", cki * 64 - 1)])
        x2T_t[cki] = sy.emit(lambda e, ss=x2lnT_d.rearrange("(c p) t -> p c t", p=128)[:, :, cki * 512:(cki + 1) * 512]:
                             e.dma_start(out=x2Tv[:, :, :], in_=ss), Lxx, 16)
        for icg in range(32):
            gic = cki * 32 + icg
            rs = icg % 2
            if gic >= 2:
                g.wait(SP, f_pe[("f1", (gic - 2) * 2 + 1)])
            io = icg * GI
            g.emit(lambda e, r=rs, ss=w_gate.rearrange("(c p) i -> p c i", p=128)[:, :, io:io + GI]:
                   e.dma_start(out=wguv[:, 0, r], in_=ss), Lgu[rs], 16)
            f_w[gic] = g.emit(lambda e, r=rs, ss=w_up.rearrange("(c p) i -> p c i", p=128)[:, :, io:io + GI]:
                              e.dma_start(out=wguv[:, 1, r], in_=ss), Lgu[rs], 16)
            for ii in range(2):
                ic = icg * 2 + ii
                gfull = cki * 64 + ic
                bg, bu = (4, 5) if ic % 2 == 0 else (2, 3)
                t.wait(Lgu[rs], f_w[gic])
                t.wait(Lxx, x2T_t[cki])
                for bkk, wsel in ((bg, 0), (bu, 1)):
                    bank_wait(t, bkk)
                    for c in range(NCH):
                        t.emit(lambda e, b=bkk, w=wsel, r=rs, cc=c, i2=ii:
                               e.matmul(pb[b][:], wguv[:, w, r, cc, i2 * 128:(i2 + 1) * 128],
                                        x2Tv[:, cc, :], start=(cc == 0), stop=(cc == NCH - 1)),
                               SP if c == NCH - 1 else None, 1)
                f_pe[("f1", gic * 2 + ii)] = SP.n
                rr = ic % 2
                a.wait(SP, SP.n)
                if gfull >= 2:
                    a.wait(SV, f_dv[gfull - 2])
                ga = a.emit(lambda e, r2_=rr, icc=ic, b=bg: e.activation(gsb[:, r2_ * 512:(r2_ + 1) * 512], pb[b][:],
                                                                   AF.Silu, scale=sgc[:, icc:icc + 1]), SA, 1)
                bank_free[bg] = (SA, ga)
                v.wait(SP, SP.n)
                if gfull >= 2:
                    v.wait(SV, f_dv[gfull - 2])
                ua = v.emit(lambda e, r2_=rr, icc=ic, b=bu: e.tensor_scalar(usb[:, r2_ * 512:(r2_ + 1) * 512], pb[b][:],
                                                                      suc[:, icc:icc + 1], None, op0=ALU.mult), SV, 1)
                bank_free[bu] = (SV, ua)
                v.wait(SA, ga)
                v.wait(SV, ua)
                if cki >= 1 and ic < 2:
                    v.wait(SP, f_pe[("f2", cki - 1)])
                f_dv[gfull] = v.emit(lambda e, r2_=rr, icc=ic:
                                     e.tensor_tensor(fusev[:, icc], gsb[:, r2_ * 512:(r2_ + 1) * 512],
                                                     usb[:, r2_ * 512:(r2_ + 1) * 512], op=ALU.mult), SV, 1)
        for h in range(2):
            for icp in range(NIC // 2):
                gdn = (cki * 2 + h) * (NIC // 2) + icp
                rs = icp % 2
                if gdn >= 2:
                    g.wait(SP, f_pe[("f2g", gdn - 2)])
                f_w[("d", gdn)] = g.emit(
                    lambda e, r=rs,
                    ss=w_down[icp * 256:(icp + 1) * 256, h * 1024:(h + 1) * 1024].rearrange("(n p) (u d) -> p n u d", p=128, u=2):
                    e.dma_start(out=wdv[:, r], in_=ss), Lwd[rs], 16)
                t.wait(Lwd[rs], f_w[("d", gdn)])
                if icp == 0:
                    t.wait(SV, f_dv[cki * 64 + NIC - 1])
                    for b_ in range(8):
                        bank_wait(t, b_)
                for nn_ in range(2):
                    ic = icp * 2 + nn_
                    for ts_ in range(4):
                        for dc in range(2):
                            t.emit(lambda e, ph=pb[ts_ * 2 + dc], icc=ic, tt=ts_, d2=dc, r=rs, n2=nn_:
                                   e.matmul(ph[:], fusev[:, icc, tt * 128:(tt + 1) * 128], wdv[:, r, n2, d2],
                                            start=(icc == 0), stop=(icc == NIC - 1)),
                                   SP if (ts_ == 3 and dc == 1 and nn_ == 1) else None, 1)
                f_pe[("f2g", gdn)] = SP.n
            f_pe[("f2", cki)] = SP.n
            v.wait(SP, SP.n)
            for ts_ in range(4):
                for dc in range(2):
                    if cki == 1 and h == 0 and ts_ == 0 and dc == 0:
                        v.wait(SV, ydv.get(3, 0))
                    evt = v.emit(lambda e, tt=ts_, d2=dc, hh=h, ph=pb[ts_ * 2 + dc]:
                                 e.tensor_tensor(ffv[:, tt, hh * 1024 + d2 * 512:hh * 1024 + (d2 + 1) * 512],
                                                 ph[:],
                                                 sdb[:, hh * 1024 + d2 * 512:hh * 1024 + (d2 + 1) * 512],
                                                 op=ALU.mult), SV, 1)
                    bank_free[ts_ * 2 + dc] = (SV, evt)
            f_ev[(cki, h)] = SV.n
        for ts_ in range(4):
            i = cki * 4 + ts_
            r = ts_ % NR
            r2 = ts_ % 2
            sy.wait(Lx2[i % 2], p5_st[i])
            sy.wait(SV, ydv.get(i - 2, 0))
            ld = sy.emit(lambda e, dd=x2r[:, r2 * D:(r2 + 1) * D], ss=x2_d[i * 128:(i + 1) * 128, :]:
                         e.dma_start(out=dd, in_=ss), Lxr[r2], 16)
            rec = ln_stats(ffv[:, ts_, :], r, pre_waits=[(SV, f_ev[(cki, 1)])])
            if i >= 2:
                v.wait(Ly[r2], y_st[i - 2])
            t_n = ln_norm(yb[:, r2 * D:(r2 + 1) * D], ffv[:, ts_, :], r, rec)
            v.wait(Lxr[r2], ld)
            v.wait(SV, t_n)
            ydv[i] = v.emit(lambda e, dd=yb[:, r2 * D:(r2 + 1) * D], s2=x2r[:, r2 * D:(r2 + 1) * D]:
                            e.tensor_tensor(dd, dd, s2, op=ALU.add), SV, 1)
            a.wait(SV, ydv[i])
            y_st[i] = a.emit(lambda e, dd=y_out[i * 128:(i + 1) * 128, :], ss=yb[:, r2 * D:(r2 + 1) * D]:
                             e.dma_start(out=dd, in_=ss), Ly[r2], 16)

    g.wait(Ly[0], y_st[6])
    g.wait(Ly[1], y_st[7])
    sy.wait(Ly[0], y_st[6])
    sy.wait(Ly[1], y_st[7])

    n_ops = {k: len(vv) for k, vv in prog.items()}
    print("[build] ops per engine:", n_ops)

    with nc.Block() as blk:
        blk.sync(lambda e: [op(e) for op in prog["sy"]])
        blk.gpsimd(lambda e: [op(e) for op in prog["g"]])
        blk.vector(lambda e: [op(e) for op in prog["v"]])
        blk.scalar(lambda e: [op(e) for op in prog["a"]])
        blk.tensor(lambda e: [op(e) for op in prog["t"]])

    return nc


_TAB_CACHE = {}


def _full_tables():
    if "t" in _TAB_CACHE:
        return _TAB_CACHE["t"]
    import jax
    import jax.numpy as jnp
    cpu = jax.devices("cpu")[0]
    with jax.default_device(cpu):
        scale_vec = (jnp.arange(0, D, 2, dtype=jnp.float32) + 0.4 * D) / (1.4 * D)
        power = jnp.arange(S, dtype=jnp.float32) / 512
        scale = scale_vec[None, :] ** power[:, None]
        scale_dn = 1.0 / scale
        inv_freq = 1.0 / 10000.0 ** (jnp.arange(1024, dtype=jnp.float32) / 1024)
        sinusoid = jnp.arange(S, dtype=jnp.float32)[:, None] * inv_freq[None, :]
        sin = jnp.sin(sinusoid)
        cos = jnp.cos(sinusoid)
        cq = cos * scale
        sq = sin * scale
        ck = cos * scale_dn
        sk = sin * scale_dn
        n = jnp.arange(S)[:, None]
        m = jnp.arange(S)[None, :]
        Dm = jnp.where(n >= m, GAMMA ** jnp.maximum(n - m, 0).astype(jnp.float32), 0.0)
        out = tuple(np.asarray(x) for x in (cq, sq, ck, sk, Dm))
    _TAB_CACHE["t"] = out
    return out


def _host_tables(own_pos, seq_pos):
    cq, sq, ck, sk, _ = _full_tables()

    def hilo(t):
        hi = np.ascontiguousarray(t.T)
        lo = np.zeros_like(hi, dtype=bfd)
        return hi, lo

    return [hilo(x) for x in (cq[own_pos], sq[own_pos], ck[seq_pos], sk[seq_pos])]


def _host_dtiles(chunk_ids):
    _, _, _, _, Dm = _full_tables()
    out = np.zeros((TOT_ITERS * CH, CH), dtype=np.float32)
    for (j, s_), t_ in ITER_IDX.items():
        gq = chunk_ids[s_]
        out[t_ * CH:(t_ + 1) * CH, :] = Dm[gq * CH:(gq + 1) * CH, j * CH:(j + 1) * CH].T
    lo = np.zeros_like(out, dtype=bfd)
    return out, lo


def make_in_maps(X, Wq, Wk, Wv, wg, wu, wd, sg, su, sd):
    wq_p = np.ascontiguousarray(np.concatenate([Wq[:, 0::2], Wq[:, 1::2]], axis=1))
    wk_p = np.ascontiguousarray(np.concatenate([Wk[:, 0::2], Wk[:, 1::2]], axis=1))
    sg_col = np.ascontiguousarray(sg.reshape(NIC, 128).T)
    su_col = np.ascontiguousarray(su.reshape(NIC, 128).T)
    sd_bc = np.ascontiguousarray(np.broadcast_to(sd[None, :], (128, D)))
    ident = np.eye(128, dtype=bfd)
    in_maps = []
    for core in range(8):
        seq = core // 4
        cg = core % 4
        chunk_ids = [cg + 4 * s_ for s_ in range(NSLOT)]
        own_pos = np.concatenate([np.arange(g_ * CH, (g_ + 1) * CH) for g_ in chunk_ids])
        x_seq_c = X[seq]
        (qch, qcl), (qsh, qsl), (kch, kcl), (ksh, ksl) = _host_tables(own_pos, np.arange(S))
        dth, dtl = _host_dtiles(chunk_ids)
        in_maps.append({
            "x_own": np.ascontiguousarray(x_seq_c[own_pos]),
            "x_seq": np.ascontiguousarray(x_seq_c),
            "wq": wq_p, "wk": wk_p, "wv": np.ascontiguousarray(Wv),
            "w_gate": wg, "w_up": wu, "w_down": wd,
            "sg_col": sg_col, "su_col": su_col, "sd_b": sd_bc,
            "ident_in": ident,
            "tqc_hi": np.ascontiguousarray(qch), "tqs_hi": np.ascontiguousarray(qsh),
            "tqc_lo": np.ascontiguousarray(qcl), "tqs_lo": np.ascontiguousarray(qsl),
            "tkc_hi": np.ascontiguousarray(kch), "tks_hi": np.ascontiguousarray(ksh),
            "tkc_lo": np.ascontiguousarray(kcl), "tks_lo": np.ascontiguousarray(ksl),
            "dt_hi": dth, "dt_lo": dtl,
        })
    return in_maps


def scatter_out(results):
    y = np.zeros((B, S, D), dtype=results[0]["y_out"].dtype)
    for core in range(8):
        seq = core // 4
        cg = core % 4
        for s_ in range(NSLOT):
            g_ = cg + 4 * s_
            y[seq, g_ * CH:(g_ + 1) * CH, :] = results[core]["y_out"][s_ * CH:(s_ + 1) * CH, :]
    return y


_NC_CACHE = {}


def kernel(L0_attn_X, L0_attn_W_Q, L0_attn_W_K, L0_attn_W_V,
           L0_ffn_w_gate, L0_ffn_w_up, L0_ffn_w_down,
           L0_ffn_s_gate, L0_ffn_s_up, L0_ffn_s_down):
    X = np.ascontiguousarray(np.asarray(L0_attn_X, dtype=np.float32))
    in_maps = make_in_maps(
        X,
        np.asarray(L0_attn_W_Q, dtype=np.float32), np.asarray(L0_attn_W_K, dtype=np.float32),
        np.asarray(L0_attn_W_V, dtype=np.float32),
        np.ascontiguousarray(np.asarray(L0_ffn_w_gate, dtype=np.int8)),
        np.ascontiguousarray(np.asarray(L0_ffn_w_up, dtype=np.int8)),
        np.ascontiguousarray(np.asarray(L0_ffn_w_down, dtype=np.int8)),
        np.asarray(L0_ffn_s_gate, dtype=np.float32), np.asarray(L0_ffn_s_up, dtype=np.float32),
        np.asarray(L0_ffn_s_down, dtype=np.float32))

    if 'nc' not in _NC_CACHE:
        _NC_CACHE['nc'] = build_nc()
    nc = _NC_CACHE['nc']
    res = run_bass_kernel_spmd(nc, in_maps, core_ids=list(range(8)))
    return scatter_out(res.results)
